# version 07
# baseline (speedup 1.0000x reference)
import numpy as np
import ml_dtypes
import concourse.bass as bass
import concourse.mybir as mybir
from concourse.bass_utils import run_bass_kernel_spmd

F32 = mybir.dt.float32
BF16 = mybir.dt.bfloat16
A = mybir.AluOpType
AF = mybir.ActivationFunctionType
AX = mybir.AxisListType.X
bfloat16 = ml_dtypes.bfloat16

D = 2048
S = 8192
TQ = 2048
HEADS = 16
KVC = 512
QC = 1024
HF = 8192
NKC = S // 128
MAGIC = 12582912.0
ATTN_ONLY = False
TRUNC = None
AB_N = 3
NO_GAMMA = False
NO_OTSPILL = False

ENGS = ("sync", "tensor", "vector", "scalar", "gpsimd")
SWAPMASK = []
for _i in range(16):
    SWAPMASK += [2 * _i + 1, 2 * _i]
KB = 1024


class Ctr:
    def __init__(self, name):
        self.name = name
        self.v = 0
        self.h = None


class Builder:
    def __init__(self):
        self.nc = bass.Bass("TRN2", target_bir_lowering=False,
                    detect_race_conditions=False)
        self.ops = {e: [] for e in ENGS}
        self.lastw = {}
        self.dS, self.dA, self.dG = Ctr("dS"), Ctr("dA"), Ctr("dG")
        self.cP, self.cA, self.cV, self.cG = Ctr("cP"), Ctr("cA"), Ctr("cV"), Ctr("cG")
        self.ctrs = [self.dS, self.dA, self.dG, self.cP, self.cA, self.cV, self.cG]

    def emit(self, eng, fn, waits=(), inc=None, by=1):
        ws = []
        for c, v in waits:
            if v <= 0:
                continue
            k = (eng, c.name)
            if self.lastw.get(k, -1) >= v:
                continue
            self.lastw[k] = v
            ws.append((c, v))
        self.ops[eng].append((ws, fn, inc, by))
        if inc is not None:
            inc.v += by
            return inc.v
        return None

    def fence(self, eng):
        self.emit(eng, lambda e: e.nop(), waits=[(c, c.v) for c in self.ctrs])

    def fence_all(self):
        for eng in ENGS:
            self.fence(eng)

    def finish(self):
        nc = self.nc
        guards = []
        for c in self.ctrs:
            g = nc.semaphore(c.name)
            c.h = g.__enter__()
            guards.append(g)
        with nc.Block() as block:
            def mk(eng_name):
                need_drain = eng_name in ("vector", "gpsimd", "scalar")
                always_drain = eng_name == "vector"
                def f(eng):
                    for ws, fn, inc, by in self.ops[eng_name]:
                        for c, v in ws:
                            eng.wait_ge(c.h, v)
                        r = fn(eng)
                        if inc is not None and (by == 16 or not need_drain):
                            r.then_inc(inc.h, by)
                        elif inc is not None:
                            eng.drain().then_inc(inc.h, by)
                        elif always_drain and r is not None:
                            eng.drain()
                return f
            block.sync(mk("sync"))
            block.tensor(mk("tensor"))
            block.vector(mk("vector"))
            block.scalar(mk("scalar"))
            block.gpsimd(mk("gpsimd"))
        for g in guards:
            g.__exit__(None, None, None)
        return nc


def build_program():
    B = Builder()
    nc = B.nc
    dS, dA, dG = B.dS, B.dA, B.dG
    cP, cA, cV, cG = B.cP, B.cA, B.cV, B.cG
    emit = B.emit

    def din(name, shape, dtype=F32):
        return nc.dram_tensor(name, list(shape), dtype, kind="ExternalInput")

    x_be = din("x_be", [S, D])
    x_q = din("x_q", [TQ, D])
    w_kvd = din("w_kvd", [D, KVC])
    w_kuk = din("w_kuk", [KVC, D])
    w_kuv = din("w_kuv", [KVC, D])
    w_qd = din("w_qd", [D, QC])
    w_qu = din("w_qu", [QC, D])
    w_kr = din("w_kr", [D, 128])
    w_qr = din("w_qr", [D, 128])
    w_out = din("w_out", [D, D])
    gw = din("gw", [HF, D])
    gbias = din("gbias", [1, HF])
    vw = din("vw", [HF, D])
    vbias = din("vbias", [1, HF])
    ow = din("ow", [D, HF])
    obias = din("obias", [1, D])
    ropec = din("ropec", [128, S])
    ropes = din("ropes", [128, S])
    ropeqc = din("ropeqc", [128, TQ])
    ropeqs = din("ropeqs", [128, TQ])
    ident = din("ident", [128, 128], BF16)
    ones_col = din("ones_col", [128, 1])
    ones_row = din("ones_row", [1, 128])

    out_t = nc.dram_tensor("out", [TQ, D], F32, kind="ExternalOutput")
    x1_dram = nc.dram_tensor("x1_scratch", [16, 128, D], F32,
                             kind="ExternalOutput" if TRUNC == "p2" else "Internal")
    gv_dram = nc.dram_tensor("gv_scratch", [16, 128, HF], F32,
                             kind="ExternalOutput" if TRUNC in ("f1", "f1w") else "Internal")
    ot_dram = nc.dram_tensor("ot_scratch", [HEADS, 128, TQ], BF16,
                             kind="ExternalOutput" if TRUNC == "p1" else "Internal")
    if TRUNC == "f1w":
        dbg_gt = nc.dram_tensor("dbg_gt", [16, 128, 8192], BF16, kind="ExternalOutput")
        dbg_vt = nc.dram_tensor("dbg_vt", [16, 128, 8192], BF16, kind="ExternalOutput")
        dbg_g3 = nc.dram_tensor("dbg_g3", [128, 16], F32, kind="ExternalOutput")
    if TRUNC == "h":
        dbg_h = nc.dram_tensor("dbg_h", [128, 2 * 8192], BF16, kind="ExternalOutput")
        dbg_sm = nc.dram_tensor("dbg_sm", [128, 32], F32, kind="ExternalOutput")
    if TRUNC == "p1ab":
        dbg_kt = nc.dram_tensor("dbg_kt", [128, S], BF16, kind="ExternalOutput")
        dbg_v = nc.dram_tensor("dbg_v", [128, S], BF16, kind="ExternalOutput")
        dbg_qt = nc.dram_tensor("dbg_qt", [128, TQ], BF16, kind="ExternalOutput")
    if TRUNC == "p0q":
        dbg_kv = nc.dram_tensor("dbg_kv", [128, 4 * S], BF16, kind="ExternalOutput")
        dbg_kr = nc.dram_tensor("dbg_kr", [128, S], BF16, kind="ExternalOutput")
        dbg_q = nc.dram_tensor("dbg_q", [128, 8 * TQ], BF16, kind="ExternalOutput")
        dbg_qr = nc.dram_tensor("dbg_qr", [128, TQ], BF16, kind="ExternalOutput")

    idn = nc.alloc_sbuf_tensor("idn", [128, 128], BF16)
    onec = nc.alloc_sbuf_tensor("onec", [128, 1], F32)
    oner = nc.alloc_sbuf_tensor("oner", [1, 128], F32)
    g3 = nc.alloc_sbuf_tensor("g3", [1, 8], F32)
    g3b = nc.alloc_sbuf_tensor("g3b", [128, 16], F32)
    gammax = nc.alloc_sbuf_tensor("gammax", [128, 16], F32)
    cg_sc = nc.alloc_sbuf_tensor("cg_sc", [128, 16], F32)
    cv_sc = nc.alloc_sbuf_tensor("cv_sc", [128, 16], F32)
    c3_sc = nc.alloc_sbuf_tensor("c3_sc", [128, 16], F32)
    gvmax = nc.alloc_sbuf_tensor("gvmax", [128, 256], F32)
    gacc = nc.alloc_sbuf_tensor("gacc", [128, 4], F32)
    trash = nc.alloc_sbuf_tensor("trash", [128, 1024], F32)
    brow = nc.alloc_sbuf_tensor("brow", [1, 1024], F32)

    arena_bytes = nc.sbuf_bytes_remaining // 32 * 32 - 32
    arena = nc.alloc_sbuf_tensor("ARENA", [128, arena_bytes], mybir.dt.uint8)
    abase = nc.lookup_mloc(arena).addr
    assert arena_bytes >= 196 * KB, f"arena too small: {arena_bytes}"

    def place(name, off, shape, dtype):
        per_part = int(np.prod(shape[1:])) * mybir.dt.size(dtype)
        assert off + per_part <= arena_bytes, (name, off, per_part, arena_bytes)
        return nc.alloc_sbuf_tensor_at(name, list(shape), dtype, offset=abase + off)

    smalls = place("smalls", 196 * KB, [128, 32], F32)
    kv_latT = place("kv_latT", 0, [128, 4 * S], BF16)
    k_rT = place("k_rT", 64 * KB, [128, S], BF16)
    q_latT = place("q_latT", 80 * KB, [128, 8 * TQ], BF16)
    qt_r = place("qt_r", 112 * KB, [128, TQ], BF16)
    hTq = place("hTq", 0, [128, 2 * 8192], BF16)
    ropebq = place("ropebq", 32 * KB, [128, 1024], F32)
    xbq = place("xbq", 36 * KB, [128, 2 * D], F32)
    hbfq = place("hbfq", 52 * KB, [128, 2 * D], BF16)
    qd_sb = place("qd", 116 * KB, [128, 16 * QC], BF16)
    qr_sb = place("qr", 148 * KB, [128, 16 * 128], BF16)
    kvd_sb = place("kvd", 116 * KB, [128, 16 * KVC], BF16)
    kr_sb = place("kr", 132 * KB, [128, 16 * 128], BF16)
    xb0 = place("xb0", 136 * KB, [128, 2 * D], F32)
    hbf0 = place("hbf0", 152 * KB, [128, 2 * D], BF16)
    hT0 = place("hT0", 160 * KB, [128, 2 * 8192], BF16)
    ropeb0 = place("ropeb0", 192 * KB, [128, 1024], F32)
    kuk_h = place("kuk_h", 116 * KB, [128, 2 * 512], BF16)
    kuv_h = place("kuv_h", 118 * KB, [128, 2 * 512], BF16)
    qu_h = place("qu_h", 120 * KB, [128, 2 * 1024], BF16)
    ktc = place("ktc", 124 * KB, [128, S], BF16)
    v_sb = place("v_sb", 140 * KB, [128, S], BF16)
    qtc = place("qtc", 156 * KB, [128, TQ], BF16)
    pt = place("pt", 160 * KB, [128, 2048], BF16)
    acc = place("acc", 164 * KB, [128, 1024], F32)
    inv_row = place("inv_row", 168 * KB, [1, 1024], F32)
    inv_bc = place("inv_bc", 172 * KB, [128, 1024], F32)
    osb = place("osb", 176 * KB, [128, 1024], BF16)
    gstage = place("gstage", 180 * KB, [128, 2 * 2048], F32)
    wout_sb = place("wout", 0, [128, 16 * D], BF16)
    otqc = place("otqc", 64 * KB, [128, 2 * 2048], BF16)
    xq_sb = place("xq_sb", 72 * KB, [128, 2 * D], F32)
    x1t = place("x1t", 88 * KB, [128, D], F32)
    h2t = place("h2t", 96 * KB, [128, D], F32)
    ntok = place("ntok", 104 * KB, [128, D], BF16)
    nT = place("nT", 108 * KB, [128, 16 * TQ], BF16)
    gwT = place("gwT", 0, [128, 2 * 16 * 512], BF16)
    vwT = place("vwT", 32 * KB, [128, 2 * 16 * 512], BF16)
    natst = place("natst", 64 * KB, [128, 2 * D], F32)
    qnat = place("qnat", 80 * KB, [128, 2 * D], BF16)
    natst2 = place("natst2", 172 * KB, [128, 2 * D], F32)
    qnat2 = place("qnat2", 188 * KB, [128, 2 * D], BF16)
    biasb = place("biasb", 88 * KB, [128, 2 * 1024], F32)
    gvt = place("gvt", 96 * KB, [128, 4 * 512], F32)
    w3T = place("w3T", 0, [128, 64 * 1024], BF16)
    b3bc = place("b3bc", 128 * KB, [128, 1024], F32)
    natfb = place("natfb", 132 * KB, [128, 2 * D], F32)
    natfb2 = place("natfb2", 152 * KB, [128, 2 * D], F32)
    x1rd = place("x1rd", 132 * KB, [128, 2 * 1024], F32)
    outbuf = place("outbuf", 140 * KB, [128, 2 * 1024], F32)
    qnfb = place("qnfb", 148 * KB, [128, D], BF16)
    qnfb2 = place("qnfb2", 188 * KB, [128, D], BF16)
    gvrd = place("gvrd", 152 * KB, [128, 2 * D], F32)
    n2q = place("n2q", 168 * KB, [128, D], BF16)
    n2T = place("n2T", 172 * KB, [128, 64 * 128], BF16)

    psA_g = nc.psum_tensor("psA", [128, 2048], F32)
    psA = psA_g.__enter__()
    psTR_g = nc.psum_tensor("psTR", [128, 2048], BF16)
    psTR = psTR_g.__enter__()
    TRH = {"t": psTR}
    psX0_g = nc.psum_tensor("psX0", [128, 1024], F32)
    SQD = {"t": psX0_g.__enter__()}

    emit("sync", lambda e: e.dma_start(idn[:, :], ident[:, :]), inc=dS, by=16)
    emit("sync", lambda e: e.dma_start(onec[:, :], ones_col[:, :]), inc=dS, by=16)
    emit("sync", lambda e: e.dma_start(oner[:, :], ones_row[:, :]), inc=dS, by=16)
    consts_dS = dS.v
    emit("gpsimd", lambda e: e.memset(gacc[:, :], 0.0), inc=cG)

    def load_chunked(dst, src, R, coloff, C, extra_waits=()):
        n = R // 128
        view = src[:, coloff:coloff + C].rearrange("(n p) c -> p n c", p=128)
        def f(e, dst=dst, view=view, n=n, C=C):
            return e.dma_start(dst[:, 0:n * C].rearrange("p (n c) -> p n c", n=n), view)
        return emit("gpsimd", f, waits=extra_waits, inc=dG, by=16)

    st = {
        "xb_cons": [0, 0],
        "xb_consA": [0, 0],
        "hbf_cons": [0, 0],
        "trb": 0,
        "trb_cons": [0, 0],
        "wq": 0,
        "psA_cons": {},
        "rope_cons": 0,
        "osb_dma": 0,
        "oevac": [0, 0],
    }

    def ln_tile(xv, hv, par, extra_waits):
        c0 = par * 12
        dmp = SQD["t"]
        emit("scalar", (lambda e, xv=xv, c0=c0, dmp=dmp: e.activation(
            dmp[:, :], xv[:, 0:1024], AF.Square,
            accum_out=smalls[:, c0 + 1:c0 + 2])), waits=extra_waits)
        emit("scalar", (lambda e, xv=xv, c0=c0, dmp=dmp: e.activation(
            dmp[:, :], xv[:, 1024:2048], AF.Square,
            accum_out=smalls[:, c0 + 10:c0 + 11])), inc=cA)
        sq_cA = cA.v
        emit("vector", (lambda e, xv=xv, c0=c0: e.reduce_sum(
            smalls[:, c0:c0 + 1], xv[:, :], AX)), waits=extra_waits)
        emit("vector", (lambda e, c0=c0: e.tensor_add(
            smalls[:, c0 + 1:c0 + 2], smalls[:, c0 + 1:c0 + 2],
            smalls[:, c0 + 10:c0 + 11])), waits=[(cA, sq_cA)])
        emit("vector", (lambda e, c0=c0: e.tensor_scalar(
            smalls[:, c0 + 2:c0 + 3], smalls[:, c0:c0 + 1], 1.0 / D, None, A.mult)))
        emit("vector", (lambda e, c0=c0: e.tensor_mul(
            smalls[:, c0 + 3:c0 + 4], smalls[:, c0 + 2:c0 + 3],
            smalls[:, c0 + 2:c0 + 3])))
        emit("vector", (lambda e, c0=c0: e.tensor_scalar(
            smalls[:, c0 + 4:c0 + 5], smalls[:, c0 + 1:c0 + 2], 1.0 / D, None,
            A.mult)))
        emit("vector", (lambda e, c0=c0: e.tensor_sub(
            smalls[:, c0 + 5:c0 + 6], smalls[:, c0 + 4:c0 + 5],
            smalls[:, c0 + 3:c0 + 4])))
        emit("vector", (lambda e, c0=c0: e.tensor_scalar(
            smalls[:, c0 + 6:c0 + 7], smalls[:, c0 + 5:c0 + 6], 1e-5, None,
            A.add)))
        emit("vector", lambda e: e.nop(), inc=cV)
        veps_cV = cV.v
        emit("scalar", (lambda e, c0=c0: e.activation(
            smalls[:, c0 + 7:c0 + 8], smalls[:, c0 + 6:c0 + 7], AF.Ln)),
            waits=[(cV, veps_cV)])
        emit("scalar", (lambda e, c0=c0: e.activation(
            smalls[:, c0 + 8:c0 + 9], smalls[:, c0 + 7:c0 + 8], AF.Exp,
            scale=-0.5)), inc=cA)
        inv_cA = cA.v
        return c0, sq_cA, inv_cA

    def ln_norm(xv, hv, c0, inv_cA, pe_wait):
        emit("vector", (lambda e, xv=xv, hv=hv, c0=c0: e.tensor_scalar(
            hv[:, :], xv[:, :], smalls[:, c0 + 2:c0 + 3],
            smalls[:, c0 + 8:c0 + 9], A.subtract, A.mult)),
            waits=[(cA, inv_cA)] + pe_wait, inc=cV)
        return cV.v

    def transpose_tile(hv, dstH, chunk_base, dsc, col, src_cV):
        nch = int(np.prod(dstH.shape[1:])) // dsc
        trh = TRH["t"]
        for dc2 in range(2):
            bk = st["trb"]
            st["trb"] ^= 1
            pb = trh[:, bk * 1024:(bk + 1) * 1024]
            def tr(e, hv=hv, pb=pb, dc2=dc2):
                r = None
                for j in range(8):
                    dc = dc2 * 8 + j
                    r = e.transpose(pb[:, j * 128:(j + 1) * 128],
                                    hv[:, dc * 128:(dc + 1) * 128], idn[:, :])
                return r
            emit("tensor", tr,
                 waits=[(cV, src_cV), (cA, st["trb_cons"][bk]), (dS, consts_dS)],
                 inc=cP)
            tr_cP = cP.v
            def ev(e, dstH=dstH, pb=pb, dc2=dc2, col=col, dsc=dsc, nch=nch,
                   chunk_base=chunk_base):
                cb = chunk_base + dc2 * 8
                dst = dstH[:, :].rearrange("p (n c) -> p n c", n=nch)[
                    :, cb:cb + 8, col:col + 128]
                return e.copy(dst, pb[:, :].rearrange("p (n c) -> p n c", n=8))
            emit("scalar", ev, waits=[(cP, tr_cP)], inc=cA)
            st["trb_cons"][bk] = cA.v
        return cA.v

    def emit_rope(bank, ropeb, dst, tw, mr_cP, rope_dS):
        t1 = trash[:, 0:512]
        t2 = trash[:, 512:1024]
        emit("vector", (lambda e, bank=bank, t1=t1: e.stream_shuffle(
            t1[:, :], bank[:, :], SWAPMASK)),
            waits=[(cP, mr_cP), (dS, rope_dS)], inc=cV)
        emit("vector", (lambda e, t1=t1, ropeb=ropeb: e.tensor_mul(
            t1[:, :], t1[:, :], ropeb[:, 512:1024])))
        emit("vector", (lambda e, bank=bank, t2=t2, ropeb=ropeb: e.tensor_mul(
            t2[:, :], bank[:, :], ropeb[:, 0:512])), inc=cV)
        emit("vector", (lambda e, dst=dst, tw=tw, t1=t1, t2=t2: e.tensor_add(
            dst[:, tw * 512:(tw + 1) * 512], t2[:, :], t1[:, :])), inc=cV)
        st["rope_cons"] = cV.v

    def p0_like(nt, x_src, per_window, rc_src, rs_src, xbuf, hbuf, hTh, ropeb,
                first_wait=()):
        first = [True]
        for tw in range(nt):
            b = tw % 2
            hTv = hTh[:, b * 8192:(b + 1) * 8192]
            emit("sync", (lambda e, tw=tw, rc_src=rc_src, ropeb=ropeb: e.dma_start(
                ropeb[:, 0:512], rc_src[:, tw * 512:(tw + 1) * 512])),
                waits=[(cV, st["rope_cons"])], inc=dS, by=16)
            emit("sync", (lambda e, tw=tw, rs_src=rs_src, ropeb=ropeb: e.dma_start(
                ropeb[:, 512:1024], rs_src[:, tw * 512:(tw + 1) * 512])),
                inc=dS, by=16)
            rope_dS = dS.v
            for t in range(4):
                g = tw * 4 + t
                gb_ = g % 2
                xv = xbuf[:, gb_ * D:(gb_ + 1) * D]
                hv = hbuf[:, gb_ * D:(gb_ + 1) * D]
                fw = list(first_wait) if first[0] else []
                first[0] = False
                emit("sync", (lambda e, xv=xv, g=g, x_src=x_src: e.dma_start(
                    xv[:, :], x_src[g * 128:(g + 1) * 128, :])),
                    waits=[(cV, st["xb_cons"][gb_]),
                           (cA, st["xb_consA"][gb_])] + fw, inc=dS, by=16)
                x_dS = dS.v
                c0, sq_cA, inv_cA = ln_tile(xv, hv, gb_, [(dS, x_dS)])
                st["xb_consA"][gb_] = sq_cA
                norm_cV = ln_norm(xv, hv, c0, inv_cA,
                                  [(cP, st["hbf_cons"][gb_])])
                st["xb_cons"][gb_] = norm_cV
                transpose_tile(hv, hTh, b * 16, 512, t * 128, norm_cV)
                st["hbf_cons"][gb_] = cP.v
            hT_dA = cA.v
            per_window(tw, hTv, hT_dA, rope_dS, ropeb)

    def proj_window(tw, hTv, hT_dA, rope_dS, ropeb, wsb, wC, nlat, dstT, dst_S,
                    wd, rope_wsb, rope_dst):
        for lc in range(nlat):
            bank = psA[:, (lc % 2) * 512:(lc % 2 + 1) * 512]
            def mm(e, hTv=hTv, lc=lc, bank=bank, wsb=wsb, wC=wC):
                r = None
                for dc in range(16):
                    r = e.matmul(bank[:, :],
                                 wsb[:, dc * wC + lc * 128:][:, 0:128],
                                 hTv[:, dc * 512:(dc + 1) * 512],
                                 start=(dc == 0), stop=(dc == 15))
                return r
            w = [(cA, hT_dA), (dG, wd)]
            pc = st["psA_cons"].get(lc % 2)
            if pc is not None:
                w.append(pc)
            emit("tensor", mm, waits=w, inc=cP)
            mm_cP = cP.v
            emit("vector", (lambda e, lc=lc, tw=tw, bank=bank, dstT=dstT,
                            dst_S=dst_S: e.tensor_copy(
                                dstT[:, lc * dst_S + tw * 512:][:, 0:512], bank[:, :])),
                 waits=[(cP, mm_cP)], inc=cV)
            st["psA_cons"][lc % 2] = (cV, cV.v)
        bank = psA[:, 1024:1536]
        def mmr(e, hTv=hTv, bank=bank, rope_wsb=rope_wsb):
            r = None
            for dc in range(16):
                r = e.matmul(bank[:, :], rope_wsb[:, dc * 128:(dc + 1) * 128],
                             hTv[:, dc * 512:(dc + 1) * 512],
                             start=(dc == 0), stop=(dc == 15))
            return r
        w = [(cA, hT_dA), (dG, wd)]
        pc = st["psA_cons"].get(2)
        if pc is not None:
            w.append(pc)
        emit("tensor", mmr, waits=w, inc=cP)
        emit_rope(bank, ropeb, rope_dst, tw, cP.v, rope_dS)
        st["psA_cons"][2] = (cV, cV.v)

    load_chunked(qd_sb, w_qd, D, 0, QC)
    w0q_d = load_chunked(qr_sb, w_qr, D, 0, 128)
    p0_like(4, x_q,
            lambda tw, hTv, hT_dA, rope_dS, ropeb: proj_window(
                tw, hTv, hT_dA, rope_dS, ropeb, qd_sb, QC, 8, q_latT, TQ,
                w0q_d, qr_sb, qt_r),
            ropeqc, ropeqs, xbq, hbfq, hTq, ropebq)

    load_chunked(kvd_sb, w_kvd, D, 0, KVC, extra_waits=[(cP, cP.v)])
    w0_d = load_chunked(kr_sb, w_kr, D, 0, 128)
    st["xb_cons"] = [0, 0]
    st["xb_consA"] = [0, 0]
    st["hbf_cons"] = [0, 0]
    st["rope_cons"] = cV.v
    p0_like(16, x_be,
            lambda tw, hTv, hT_dA, rope_dS, ropeb: proj_window(
                tw, hTv, hT_dA, rope_dS, ropeb, kvd_sb, KVC, 4, kv_latT, S,
                w0_d, kr_sb, k_rT),
            ropec, ropes, xb0, hbf0, hT0, ropeb0,
            first_wait=[(cP, cP.v)])

    if TRUNC == "h":
        emit("scalar", lambda e: e.dma_start(dbg_h[:, :], hT0[:, :]),
             waits=[(cV, cV.v), (cA, cA.v), (cP, cP.v)], inc=dA, by=16)
        emit("scalar", lambda e: e.dma_start(dbg_sm[:, :], smalls[:, :]), inc=dA, by=16)
        emit("sync", lambda e: e.nop(),
             waits=[(dA, dA.v), (dS, dS.v), (dG, dG.v), (cP, cP.v), (cA, cA.v),
                    (cV, cV.v), (cG, cG.v)])
        return B.finish()

    def finish_trunc():
        emit("sync", lambda e: e.nop(),
             waits=[(dA, dA.v), (dS, dS.v), (dG, dG.v), (cP, cP.v), (cA, cA.v),
                    (cV, cV.v), (cG, cG.v)])
        return B.finish()

    if TRUNC == "p0q":
        emit("scalar", lambda e: e.dma_start(dbg_kv[:, :], kv_latT[:, :]),
             waits=[(cV, cV.v), (cA, cA.v), (cP, cP.v)], inc=dA, by=16)
        emit("scalar", lambda e: e.dma_start(dbg_kr[:, :], k_rT[:, :]), inc=dA, by=16)
        emit("scalar", lambda e: e.dma_start(dbg_q[:, :], q_latT[:, :]), inc=dA, by=16)
        emit("scalar", lambda e: e.dma_start(dbg_qr[:, :], qt_r[:, :]), inc=dA, by=16)
        return finish_trunc()

    gsrcs = [("g", i) for i in range(64)] + [("v", i) for i in range(64)] + \
            [("o", i) for i in range(64)]
    gstate = {"idx": 0, "cons": [0, 0]}

    def emit_gamma_tiles(n):
        for _ in range(n):
            if gstate["idx"] >= len(gsrcs):
                return
            kind, ti = gsrcs[gstate["idx"]]
            b = gstate["idx"] % 2
            gstate["idx"] += 1
            gv_ = gstage[:, b * 2048:(b + 1) * 2048]
            mat = {"g": 0, "v": 1, "o": 2}[kind]
            src = {"g": gw, "v": vw, "o": ow}[kind]
            if kind != "o":
                def ld(e, gv_=gv_, src=src, ti=ti):
                    return e.dma_start(gv_[:, :], src[ti * 128:(ti + 1) * 128, :])
            else:
                def ld(e, gv_=gv_, src=src, ti=ti):
                    r, q = ti // 4, ti % 4
                    return e.dma_start(
                        gv_[:, :], src[r * 128:(r + 1) * 128, q * 2048:(q + 1) * 2048])
            w = [(cV, gstate["cons"][b])]
            if gstate["idx"] <= 2:
                w += [(cP, cP.v), (cV, cV.v), (cA, cA.v)]
            emit("sync", ld, waits=w, inc=dS, by=16)
            t_dS = dS.v
            emit("vector", (lambda e, gv_=gv_: e.tensor_reduce(
                smalls[:, 29:30], gv_[:, :], AX, A.add, apply_absolute_value=True)),
                waits=[(dS, t_dS)], inc=cV)
            emit("vector", (lambda e, mat=mat: e.tensor_add(
                gacc[:, mat:mat + 1], gacc[:, mat:mat + 1], smalls[:, 29:30])),
                inc=cV)
            gstate["cons"][b] = cV.v

    def emit_gamma_final():
        gp_cV = cV.v
        bank = psA[0:1, 0:512]
        w = [(cV, gp_cV), (dS, consts_dS)]
        pc = st["psA_cons"].get(0)
        if pc is not None:
            w.append(pc)
        emit("tensor", (lambda e, bank=bank: e.matmul(
            bank[0:1, 0:4], onec[:, :], gacc[:, 0:4], start=True, stop=True)),
            waits=w, inc=cP)
        s_cP = cP.v
        emit("vector", (lambda e, bank=bank: e.tensor_copy(
            g3[0:1, 0:4], bank[0:1, 0:4])), waits=[(cP, s_cP)], inc=cV)
        st["psA_cons"][0] = (cV, cV.v)
        emit("vector", lambda e: e.tensor_scalar(
            g3[0:1, 0:3], g3[0:1, 0:3], 1.0 / (HF * D), 1e-5, A.mult, A.max), inc=cV)
        emit("vector", lambda e: e.reciprocal(g3[0:1, 3:6], g3[0:1, 0:3]), inc=cV)
        g_cV = cV.v
        bank2 = psA[:, 512:1024]
        w = [(cV, g_cV)]
        pc = st["psA_cons"].get(1)
        if pc is not None:
            w.append(pc)
        emit("tensor", (lambda e, bank2=bank2: e.matmul(
            bank2[:, 0:6], oner[:, :], g3[0:1, 0:6], start=True, stop=True)),
            waits=w, inc=cP)
        b_cP = cP.v
        emit("scalar", (lambda e, bank2=bank2: e.copy(g3b[:, 0:6], bank2[:, 0:6])),
             waits=[(cP, b_cP)])
        emit("scalar", (lambda e, bank2=bank2: e.activation(
            g3b[:, 6:9], bank2[:, 0:3], AF.Copy, scale=0.5)))
        emit("scalar", (lambda e, bank2=bank2: e.activation(
            g3b[:, 9:12], bank2[:, 0:3], AF.Copy, scale=-0.5)), inc=cA)
        st["psA_cons"][1] = (cA, cA.v)
        return cA.v

    SQD["t"] = None
    psX0_g.__exit__(None, None, None)
    psTR_g.__exit__(None, None, None)
    psO_g = nc.psum_tensor("psO", [128, 2048], F32)
    ps_O = psO_g.__enter__()

    wslice_d = [0, 0]

    def load_head_slices(h):
        b = h % 2
        def mkld(wdram, dstbuf, n):
            def f(e, wdram=wdram, dstbuf=dstbuf, n=n, h=h):
                view = wdram[:, h * 128:(h + 1) * 128].rearrange(
                    "(n p) c -> p n c", p=128)
                return e.dma_start(dstbuf.rearrange("p (n c) -> p n c", n=n), view)
            return f
        w = [(cP, cP.v)]
        emit("gpsimd", mkld(w_kuk, kuk_h[:, b * 512:(b + 1) * 512], 4),
             waits=w, inc=dG, by=16)
        emit("gpsimd", mkld(w_kuv, kuv_h[:, b * 512:(b + 1) * 512], 4), inc=dG, by=16)
        emit("gpsimd", mkld(w_qu, qu_h[:, b * 1024:(b + 1) * 1024], 8), inc=dG, by=16)
        wslice_d[b] = dG.v

    if not (TRUNC == "p1ab" and AB_N < 0):
        load_head_slices(0)
    exp_c = {}
    sv_c = {}
    sg_c = {}
    iter_c = {}
    psO_cons = {}

    B.fence_all()

    for h in range(HEADS):
        b = h % 2
        if h + 1 < HEADS and not (TRUNC == "p1ab" and AB_N < 0):
            load_head_slices(h + 1)
        for kw in range(16 if not (TRUNC == "p1ab" and AB_N < 1) else 0):
            bank = psA[:, (kw % 2) * 512:(kw % 2 + 1) * 512]
            def mmk(e, kw=kw, b=b, bank=bank):
                r = None
                for latc in range(4):
                    r = e.matmul(bank[:, :],
                                 kuk_h[:, b * 512 + latc * 128:][:, 0:128],
                                 kv_latT[:, latc * S + kw * 512:][:, 0:512],
                                 start=(latc == 0), stop=(latc == 3))
                return r
            w = [(dG, wslice_d[b])]
            pc = st["psA_cons"].get(kw % 2)
            if pc is not None:
                w.append(pc)
            emit("tensor", mmk, waits=w, inc=cP)
            k_cP = cP.v
            emit("scalar", (lambda e, kw=kw, bank=bank: e.copy(
                ktc[:, kw * 512:(kw + 1) * 512], bank[:, :])),
                waits=[(cP, k_cP)], inc=cA)
            st["psA_cons"][kw % 2] = (cA, cA.v)
        ktc_dA = cA.v
        for kcg in range(16 if not (TRUNC == "p1ab" and AB_N < 2) else 0):
            bank = ps_O[:, (kcg % 4) * 512:(kcg % 4 + 1) * 512]
            def mmv(e, kcg=kcg, b=b, bank=bank):
                r = None
                for j in range(4):
                    kc = kcg * 4 + j
                    for latc in range(4):
                        r = e.matmul(bank[:, j * 128:(j + 1) * 128],
                                     kv_latT[:, latc * S + kc * 128:][:, 0:128],
                                     kuv_h[:, b * 512 + latc * 128:][:, 0:128],
                                     start=(latc == 0), stop=(latc == 3))
                return r
            w = [(cV, st["oevac"][0]), (cV, st["oevac"][1])]
            pc = psO_cons.get(kcg % 4)
            if pc is not None:
                w.append(pc)
            emit("tensor", mmv, waits=w, inc=cP)
            v_cP = cP.v
            emit("scalar", (lambda e, kcg=kcg, bank=bank: e.copy(
                v_sb[:, kcg * 512:(kcg + 1) * 512], bank[:, :])),
                waits=[(cP, v_cP)], inc=cA)
            psO_cons[kcg % 4] = (cA, cA.v)
        v_dA = cA.v
        for qw in range(4 if not (TRUNC == "p1ab" and AB_N < 3) else 0):
            bank = psA[:, (qw % 2) * 512:(qw % 2 + 1) * 512]
            def mmq(e, qw=qw, b=b, bank=bank):
                r = None
                for qlc in range(8):
                    r = e.matmul(bank[:, :],
                                 qu_h[:, b * 1024 + qlc * 128:][:, 0:128],
                                 q_latT[:, qlc * TQ + qw * 512:][:, 0:512],
                                 start=(qlc == 0), stop=(qlc == 7))
                return r
            w = []
            pc = st["psA_cons"].get(qw % 2)
            if pc is not None:
                w.append(pc)
            emit("tensor", mmq, waits=w, inc=cP)
            q_cP = cP.v
            emit("scalar", (lambda e, qw=qw, bank=bank: e.copy(
                qtc[:, qw * 512:(qw + 1) * 512], bank[:, :])),
                waits=[(cP, q_cP)], inc=cA)
            st["psA_cons"][qw % 2] = (cA, cA.v)
        qtc_dA = cA.v
        if TRUNC == "p1ab" and h == 0:
            src_kt = ktc if AB_N >= 1 else k_rT
            src_v = v_sb if AB_N >= 2 else k_rT
            src_qt = qtc if AB_N >= 3 else qt_r
            emit("scalar", (lambda e, src_kt=src_kt: e.dma_start(
                dbg_kt[:, :], src_kt[:, :])),
                waits=[(cA, cA.v), (cP, cP.v), (cV, cV.v)], inc=dA, by=16)
            emit("scalar", (lambda e, src_v=src_v: e.dma_start(
                dbg_v[:, :], src_v[:, :])), inc=dA, by=16)
            emit("scalar", (lambda e, src_qt=src_qt: e.dma_start(
                dbg_qt[:, :], src_qt[:, :])), inc=dA, by=16)
            return finish_trunc()
        emit_gamma_tiles(0 if NO_GAMMA else (4 if h == 0 else 13))

        for qh in range(2):
            ov = ps_O[:, qh * 1024:(qh + 1) * 1024]
            qoff = qh * 1024
            for kc in range(NKC):
                sb_ = kc % 2
                sview = psA[:, sb_ * 1024:(sb_ + 1) * 1024]
                def mm_s(e, kc=kc, sview=sview, qoff=qoff):
                    r = None
                    for qw in range(2):
                        bv = sview[:, qw * 512:(qw + 1) * 512]
                        r = e.matmul(bv, ktc[:, kc * 128:(kc + 1) * 128],
                                     qtc[:, qoff + qw * 512:][:, 0:512],
                                     start=True, stop=False)
                        r = e.matmul(bv, k_rT[:, kc * 128:(kc + 1) * 128],
                                     qt_r[:, qoff + qw * 512:][:, 0:512],
                                     start=False, stop=True)
                    return r
                w = []
                if kc == 0:
                    w = [(cA, cA.v), (cV, cV.v), (cG, cG.v)]
                if kc >= 2:
                    w.append((cA, exp_c[(h, qh, kc - 2)]))
                emit("tensor", mm_s, waits=w, inc=(cP if kc == 0 else None))
                if kc >= 1:
                    pb_ = (kc - 1) % 2
                    def mm_av(e, kc=kc, ov=ov, pb_=pb_):
                        r = None
                        for qw in range(2):
                            r = e.matmul(ov[:, qw * 512:(qw + 1) * 512],
                                         v_sb[:, (kc - 1) * 128:kc * 128],
                                         pt[:, pb_ * 1024 + qw * 512:][:, 0:512],
                                         start=(kc - 1 == 0), stop=False)
                        return r
                    wv = [(cA, exp_c[(h, qh, kc - 1)])]
                    if kc == 1:
                        wv.append((cV, st["oevac"][qh]))
                        for k_ in (qh * 2, qh * 2 + 1):
                            pc = psO_cons.get(k_)
                            if pc is not None:
                                wv.append(pc)
                    emit("tensor", mm_av, waits=wv, inc=cP)
                iter_c[(h, qh, kc)] = cP.v
                def f_exp(e, sview=sview, sb_=sb_):
                    e.activation(pt[:, sb_ * 1024:sb_ * 1024 + 512],
                                 sview[:, 0:512], AF.Exp, scale=0.0625)
                    return e.activation(pt[:, sb_ * 1024 + 512:sb_ * 1024 + 1024],
                                        sview[:, 512:1024], AF.Exp, scale=0.0625)
                w = [(cP, iter_c[(h, qh, kc)])]
                if kc >= 2:
                    w += [(cV, sv_c[(h, qh, kc - 2)]), (cG, sg_c[(h, qh, kc - 2)])]
                emit("scalar", f_exp, waits=w, inc=cA)
                exp_c[(h, qh, kc)] = cA.v
                if kc == 0:
                    emit("vector", (lambda e, sb_=sb_: e.tensor_copy(
                        acc[:, 0:512], pt[:, sb_ * 1024:][:, 0:512])),
                        waits=[(cA, exp_c[(h, qh, kc)])], inc=cV)
                    emit("gpsimd", (lambda e, sb_=sb_: e.tensor_copy(
                        acc[:, 512:1024], pt[:, sb_ * 1024 + 512:][:, 0:512])),
                        waits=[(cA, exp_c[(h, qh, kc)])], inc=cG)
                else:
                    emit("vector", (lambda e, sb_=sb_: e.tensor_add(
                        acc[:, 0:512], acc[:, 0:512], pt[:, sb_ * 1024:][:, 0:512])),
                        waits=[(cA, exp_c[(h, qh, kc)])], inc=cV)
                    emit("gpsimd", (lambda e, sb_=sb_: e.tensor_add(
                        acc[:, 512:1024], acc[:, 512:1024],
                        pt[:, sb_ * 1024 + 512:][:, 0:512])),
                        waits=[(cA, exp_c[(h, qh, kc)])], inc=cG)
                sv_c[(h, qh, kc)] = cV.v
                sg_c[(h, qh, kc)] = cG.v
            def mm_av63(e, ov=ov):
                r = None
                for qw in range(2):
                    r = e.matmul(ov[:, qw * 512:(qw + 1) * 512],
                                 v_sb[:, (NKC - 1) * 128:NKC * 128],
                                 pt[:, ((NKC - 1) % 2) * 1024 + qw * 512:][:, 0:512],
                                 start=False, stop=True)
                return r
            emit("tensor", mm_av63, waits=[(cA, exp_c[(h, qh, NKC - 1)])], inc=cP)
            av_end = cP.v
            def mm_sum(e):
                e.matmul(psA[0:1, 0:512], onec[:, :], acc[:, 0:512],
                         start=True, stop=True)
                return e.matmul(psA[0:1, 512:1024], onec[:, :], acc[:, 512:1024],
                                start=True, stop=True)
            emit("tensor", mm_sum,
                 waits=[(cV, sv_c[(h, qh, NKC - 1)]), (cG, sg_c[(h, qh, NKC - 1)]),
                        (cA, exp_c[(h, qh, NKC - 1)])], inc=cP)
            sum_cP = cP.v
            emit("vector", lambda e: e.reciprocal(inv_row[0:1, 0:512],
                                                  psA[0:1, 0:512]),
                 waits=[(cP, sum_cP)])
            emit("vector", lambda e: e.reciprocal(inv_row[0:1, 512:1024],
                                                  psA[0:1, 512:1024]), inc=cV)
            rec_cV = cV.v
            st["psA_cons"][0] = (cV, rec_cV)
            st["psA_cons"][1] = (cV, rec_cV)
            def mm_bc(e):
                e.matmul(psA[:, 1024:1536], oner[:, :], inv_row[0:1, 0:512],
                         start=True, stop=True)
                return e.matmul(psA[:, 1536:2048], oner[:, :],
                                inv_row[0:1, 512:1024], start=True, stop=True)
            emit("tensor", mm_bc, waits=[(cV, rec_cV)], inc=cP)
            bc_cP = cP.v
            emit("scalar", lambda e: e.copy(inv_bc[:, 0:512], psA[:, 1024:1536]),
                 waits=[(cP, bc_cP)])
            emit("scalar", lambda e: e.copy(inv_bc[:, 512:1024], psA[:, 1536:2048]),
                 inc=cA)
            bce_cA = cA.v
            st["psA_cons"][2] = (cA, bce_cA)
            st["psA_cons"][3] = (cA, bce_cA)
            emit("vector", (lambda e, ov=ov: e.tensor_mul(
                osb[:, 0:512], ov[:, 0:512], inv_bc[:, 0:512])),
                waits=[(cP, av_end), (cA, bce_cA), (dA, st["osb_dma"])])
            emit("vector", (lambda e, ov=ov: e.tensor_mul(
                osb[:, 512:1024], ov[:, 512:1024], inv_bc[:, 512:1024])), inc=cV)
            st["oevac"][qh] = cV.v
            if not NO_OTSPILL:
                emit("scalar", (lambda e, h=h, qh=qh: e.dma_start(
                    ot_dram[h, :, qh * 1024:(qh + 1) * 1024], osb[:, :])),
                    waits=[(cV, st["oevac"][qh])], inc=dA, by=16)
                st["osb_dma"] = dA.v
            psO_cons[qh * 2] = (cP, av_end)
            psO_cons[qh * 2 + 1] = (cP, av_end)
        if TRUNC == f"p1h{h + 1}":
            return finish_trunc()

    g3b_dA = emit_gamma_final()
    if TRUNC == "p1":
        return finish_trunc()

    wout_d = 0
    for hc in range(16):
        def f(e, hc=hc):
            return e.dma_start(wout_sb[:, hc * D:(hc + 1) * D],
                               w_out[hc * 128:(hc + 1) * 128, :])
        emit("gpsimd", f, waits=[(cP, cP.v)], inc=dG, by=16)
    wout_d = dG.v

    B.fence_all()
    ot_done_dA = dA.v
    psO_g.__exit__(None, None, None)
    psTR2_g = nc.psum_tensor("psTR2", [128, 2048], BF16)
    TRH["t"] = psTR2_g.__enter__()
    psX_g = nc.psum_tensor("psX", [128, 1024], F32)
    psX = psX_g.__enter__()
    SQD["t"] = psX
    st["trb_cons"] = [0, 0]

    x1_spill = []
    nT_done = []
    otqc_cons = [0, 0]
    xq_cons = [0, 0]

    for qc in range(16):
        b = qc % 2
        ov_ = otqc[:, b * 2048:(b + 1) * 2048]
        xv_ = xq_sb[:, b * D:(b + 1) * D]
        def ldot(e, ov_=ov_, qc=qc):
            src = ot_dram[:, :, qc * 128:(qc + 1) * 128].rearrange("n p c -> p n c")
            return e.dma_start(ov_.rearrange("p (n c) -> p n c", n=16), src)
        emit("sync", ldot, waits=[(dA, ot_done_dA), (cP, otqc_cons[b])],
             inc=dS, by=16)
        ot_dS = dS.v
        emit("sync", (lambda e, xv_=xv_, qc=qc: e.dma_start(
            xv_[:, :], x_q[qc * 128:(qc + 1) * 128, :])),
            waits=[(cV, xq_cons[b])], inc=dS, by=16)
        xq_dS = dS.v
        def mmw(e, ov_=ov_):
            r = None
            for hc in range(16):
                for dmw in range(4):
                    r = e.matmul(psA[:, dmw * 512:(dmw + 1) * 512],
                                 ov_[:, hc * 128:(hc + 1) * 128],
                                 wout_sb[:, hc * D + dmw * 512:][:, 0:512],
                                 start=(hc == 0), stop=(hc == 15))
            return r
        w = [(dS, ot_dS), (dG, wout_d)]
        for k_ in range(4):
            pc = st["psA_cons"].get(k_)
            if pc is not None:
                w.append(pc)
        emit("tensor", mmw, waits=w, inc=cP)
        mmw_cP = cP.v
        otqc_cons[b] = mmw_cP
        for dmw in range(4):
            emit("vector", (lambda e, xv_=xv_, dmw=dmw: e.tensor_add(
                x1t[:, dmw * 512:(dmw + 1) * 512],
                psA[:, dmw * 512:(dmw + 1) * 512],
                xv_[:, dmw * 512:(dmw + 1) * 512])),
                waits=([(cP, mmw_cP), (dS, xq_dS),
                        (dA, x1_spill[-1] if x1_spill else 0)]
                       if dmw == 0 else ()),
                inc=(cV if dmw == 3 else None))
        x1_cV = cV.v
        xq_cons[b] = x1_cV
        for k_ in range(4):
            st["psA_cons"][k_] = (cV, x1_cV)
        emit("scalar", (lambda e, qc=qc: e.dma_start(
            x1_dram[qc, :, :], x1t[:, :])), waits=[(cV, x1_cV)], inc=dA, by=16)
        x1_spill.append(dA.v)
        if ATTN_ONLY:
            emit("scalar", (lambda e, qc=qc: e.dma_start(
                out_t[qc * 128:(qc + 1) * 128, :], x1t[:, :])), inc=dA, by=16)
            nT_done.append(cA.v)
            continue
        c0_, sq_cA_, inv_cA_ = ln_tile(x1t, h2t, qc % 2, [(cV, x1_cV)])
        h2_cV = ln_norm(x1t, h2t, c0_, inv_cA_, [])
        emit("vector", (lambda e, qc=qc: e.reduce_max(
            gammax[:, qc:qc + 1], h2t[:, :], AX, apply_absolute_value=True)))
        emit("vector", (lambda e, qc=qc: e.tensor_scalar(
            gammax[:, qc:qc + 1], gammax[:, qc:qc + 1], 1e-5, None, A.max)), inc=cV)
        gx_cV = cV.v
        emit("vector", (lambda e, qc=qc: e.reciprocal(
            smalls[:, 28:29], gammax[:, qc:qc + 1])), inc=cV)
        emit("vector", lambda e: e.tensor_scalar(
            trash[:, 0:1024], h2t[:, 0:1024], smalls[:, 28:29], 127.0,
            A.mult, A.mult))
        emit("vector", lambda e: e.tensor_scalar(
            ntok[:, 0:1024], trash[:, 0:1024], MAGIC, -MAGIC, A.add, A.add))
        emit("vector", lambda e: e.tensor_scalar(
            trash[:, 0:1024], h2t[:, 1024:2048], smalls[:, 28:29], 127.0,
            A.mult, A.mult))
        emit("vector", lambda e: e.tensor_scalar(
            ntok[:, 1024:2048], trash[:, 0:1024], MAGIC, -MAGIC, A.add, A.add),
            inc=cV)
        n_cV = cV.v
        emit("vector", (lambda e, qc=qc: e.tensor_scalar(
            cg_sc[:, qc:qc + 1], gammax[:, qc:qc + 1], g3b[:, 0:1], 1.0 / 127.0,
            A.mult, A.mult)), waits=[(cA, g3b_dA)])
        emit("vector", (lambda e, qc=qc: e.tensor_scalar(
            cv_sc[:, qc:qc + 1], gammax[:, qc:qc + 1], g3b[:, 1:2], 1.0 / 127.0,
            A.mult, A.mult)), inc=cV)
        nT_done.append(transpose_tile(ntok, nT, 0, TQ, qc * 128, n_cV))

    if TRUNC == "p2":
        return finish_trunc()
    if not ATTN_ONLY:
        B.fence_all()
        nat_cons = [0, 0, 0, 0]
        qn_cons = [0, 0, 0, 0]
        natb = [natst[:, 0:D], natst[:, D:2 * D],
                natst2[:, 0:D], natst2[:, D:2 * D]]
        qnb = [qnat[:, 0:D], qnat[:, D:2 * D],
               qnat2[:, 0:D], qnat2[:, D:2 * D]]
        bias_cons = [0, 0]
        brow_cons = [0]
        psF_cons = {}
        gv_spill = {}

        def quantize_nat(nv, qv, inv_col, extra_w):
            f = inv_col - 3
            emit("vector", (lambda e, nv=nv, f=f: e.tensor_scalar(
                trash[:, 0:1024], nv[:, 0:1024], g3b[:, 9 + f:10 + f],
                None, A.is_lt)), waits=extra_w)
            emit("vector", (lambda e, nv=nv, qv=qv, f=f: e.scalar_tensor_tensor(
                qv[:, 0:1024], nv[:, 0:1024], g3b[:, 6 + f:7 + f],
                trash[:, 0:1024], A.is_gt, A.subtract)))
            emit("vector", (lambda e, nv=nv, f=f: e.tensor_scalar(
                trash[:, 0:1024], nv[:, 1024:2048], g3b[:, 9 + f:10 + f],
                None, A.is_lt)))
            emit("vector", (lambda e, nv=nv, qv=qv, f=f: e.scalar_tensor_tensor(
                qv[:, 1024:2048], nv[:, 1024:2048], g3b[:, 6 + f:7 + f],
                trash[:, 0:1024], A.is_gt, A.subtract)), inc=cV)
            return cV.v

        def build_wT(hw, dstH, cb16, wdram, inv_col):
            slots = []
            for i in range(4):
                nb = st["wq"] % 4
                st["wq"] += 1
                nv = natb[nb]
                def ld(e, nv=nv, wdram=wdram, hw=hw, i=i):
                    return e.dma_start(
                        nv[:, :], wdram[hw * 512 + i * 128:hw * 512 + (i + 1) * 128, :])
                emit("sync", ld, waits=[(cV, nat_cons[nb])], inc=dS, by=16)
                slots.append((nb, dS.v))
            last = 0
            for i in range(4):
                nb, n_dS = slots[i]
                q_cV = quantize_nat(natb[nb], qnb[nb], inv_col,
                                    [(dS, n_dS), (cA, g3b_dA), (cP, qn_cons[nb])])
                nat_cons[nb] = q_cV
                last = transpose_tile(qnb[nb], dstH, cb16, 512, i * 128, q_cV)
                qn_cons[nb] = cP.v
            return last

        for hw in range(16):
            wb_ = hw % 2
            gT = gwT[:, wb_ * 8192:(wb_ + 1) * 8192]
            vT = vwT[:, wb_ * 8192:(wb_ + 1) * 8192]
            gT_dA = build_wT(hw, gwT, wb_ * 16, gw, 3)
            vT_dA = build_wT(hw, vwT, wb_ * 16, vw, 4)
            if TRUNC == "f1w":
                if hw == 0:
                    emit("gpsimd", lambda e: e.dma_start(dbg_g3[:, :], g3b[:, :]),
                         waits=[(cA, g3b_dA)], inc=dG, by=16)
                emit("gpsimd", (lambda e, hw=hw, gT=gT: e.dma_start(
                    dbg_gt[hw], gT[:, :])), waits=[(cA, gT_dA)], inc=dG, by=16)
                emit("gpsimd", (lambda e, hw=hw, vT=vT: e.dma_start(
                    dbg_vt[hw], vT[:, :])), waits=[(cA, vT_dA)], inc=dG, by=16)
            emit("sync", (lambda e, hw=hw: e.dma_start(
                brow[0:1, 0:512], gbias[0:1, hw * 512:(hw + 1) * 512])),
                waits=[(cP, brow_cons[0])], inc=dS, by=16)
            emit("sync", (lambda e, hw=hw: e.dma_start(
                brow[0:1, 512:1024], vbias[0:1, hw * 512:(hw + 1) * 512])),
                inc=dS, by=16)
            br_dS = dS.v
            def bmm(e):
                e.matmul(psX[:, 0:512], oner[:, :], brow[0:1, 0:512],
                         start=True, stop=True)
                return e.matmul(psX[:, 512:1024], oner[:, :], brow[0:1, 512:1024],
                                start=True, stop=True)
            emit("tensor", bmm, waits=[(cA, bias_cons[wb_]), (dS, br_dS)], inc=cP)
            bmm_cP = cP.v
            brow_cons[0] = bmm_cP
            emit("scalar", (lambda e, wb_=wb_: e.copy(
                biasb[:, wb_ * 1024:wb_ * 1024 + 512], psX[:, 0:512])),
                waits=[(cP, bmm_cP)])
            emit("scalar", (lambda e, wb_=wb_: e.copy(
                biasb[:, wb_ * 1024 + 512:wb_ * 1024 + 1024], psX[:, 512:1024])),
                inc=cA)
            bias_dA = cA.v
            bias_cons[wb_] = bias_dA
            for tc in range(16):
                gbank = psA[:, (tc % 2) * 512:(tc % 2 + 1) * 512]
                vbank = psA[:, 1024 + (tc % 2) * 512:1024 + (tc % 2 + 1) * 512]
                def mmgv(e, tc=tc, gT=gT, vT=vT, gbank=gbank, vbank=vbank):
                    r = None
                    for dc in range(16):
                        lhs = nT[:, dc * TQ + tc * 128:][:, 0:128]
                        r = e.matmul(gbank, lhs, gT[:, dc * 512:(dc + 1) * 512],
                                     start=(dc == 0), stop=(dc == 15))
                        r = e.matmul(vbank, lhs, vT[:, dc * 512:(dc + 1) * 512],
                                     start=(dc == 0), stop=(dc == 15))
                    return r
                w = [(cA, gT_dA), (cA, vT_dA), (cA, nT_done[tc])]
                for k_ in ((tc % 2), 2 + (tc % 2)):
                    pc = psF_cons.get(k_)
                    if pc is not None:
                        w.append(pc)
                emit("tensor", mmgv, waits=w, inc=cP)
                gv_cP = cP.v
                g1 = gvt[:, 0:512]
                gate = gvt[:, 512:1024]
                v1 = gvt[:, 1024:1536]
                gvo = gvt[:, 1536:2048]
                emit("vector", (lambda e, gbank=gbank, tc=tc, wb_=wb_, g1=g1:
                                e.scalar_tensor_tensor(
                                    g1[:, :], gbank, cg_sc[:, tc:tc + 1],
                                    biasb[:, wb_ * 1024:wb_ * 1024 + 512],
                                    A.mult, A.add)),
                     waits=[(cP, gv_cP), (cA, bias_dA),
                            (dA, gv_spill.get((hw - 1, tc), 0))], inc=cV)
                g1_cV = cV.v
                emit("scalar", (lambda e, g1=g1, gate=gate: e.activation(
                    gate[:, :], g1[:, :], AF.Exp, scale=-1.0)),
                    waits=[(cV, g1_cV)], inc=cA)
                si_cA = cA.v
                emit("vector", (lambda e, gate=gate: e.tensor_scalar(
                    gate[:, :], gate[:, :], 1.0, None, A.add)),
                    waits=[(cA, si_cA)])
                emit("vector", (lambda e, gate=gate: e.reciprocal(
                    gate[:, :], gate[:, :])))
                emit("vector", (lambda e, gate=gate, g1=g1: e.tensor_mul(
                    gate[:, :], gate[:, :], g1[:, :])), inc=cV)
                si_cA = None
                emit("vector", (lambda e, vbank=vbank, tc=tc, wb_=wb_, v1=v1:
                                e.scalar_tensor_tensor(
                                    v1[:, :], vbank, cv_sc[:, tc:tc + 1],
                                    biasb[:, wb_ * 1024 + 512:wb_ * 1024 + 1024],
                                    A.mult, A.add)), inc=cV)
                v1_cV = cV.v
                psF_cons[(tc % 2)] = (cV, g1_cV)
                psF_cons[2 + (tc % 2)] = (cV, v1_cV)
                emit("vector", (lambda e, gate=gate, v1=v1, gvo=gvo: e.tensor_mul(
                    gvo[:, :], gate[:, :], v1[:, :])),
                    waits=[(dA, gv_spill.get((hw, tc - 1),
                                             gv_spill.get((hw - 1, 15), 0)))])
                emit("vector", (lambda e, gvo=gvo, tc=tc, hw=hw: e.reduce_max(
                    gvmax[:, tc * 16 + hw:tc * 16 + hw + 1], gvo[:, :], AX,
                    apply_absolute_value=True)), inc=cV)
                gv_cV = cV.v
                emit("scalar", (lambda e, gvo=gvo, tc=tc, hw=hw: e.dma_start(
                    gv_dram[tc, :, hw * 512:(hw + 1) * 512], gvo[:, :])),
                    waits=[(cV, gv_cV)], inc=dA, by=16)
                gv_spill[(hw, tc)] = dA.v

        if TRUNC in ("f1", "f1w"):
            return finish_trunc()
        B.fence_all()
        natfb_cons = [0, 0, 0, 0]
        qnfb_cons = [0, 0]
        nfb = [natfb[:, 0:D], natfb[:, D:2 * D],
               natfb2[:, 0:D], natfb2[:, D:2 * D]]
        qfb = [qnfb, qnfb2]
        gvrd_cons = [0, 0, 0, 0]
        x1rd_cons = [0, 0]
        outb_cons = [0, 0]
        n2T_cons = 0
        psF_cons = {}

        for tc in range(16):
            emit("vector", (lambda e, tc=tc: e.reduce_max(
                smalls[:, 30:31], gvmax[:, tc * 16:(tc + 1) * 16], AX)))
            emit("vector", (lambda e, tc=tc: e.tensor_scalar(
                gvmax[:, tc * 16:tc * 16 + 1], smalls[:, 30:31], 1e-5, None, A.max)))
            emit("vector", (lambda e, tc=tc: e.tensor_scalar(
                c3_sc[:, tc:tc + 1], gvmax[:, tc * 16:tc * 16 + 1], g3b[:, 2:3],
                1.0 / 127.0, A.mult, A.mult)))
            emit("vector", (lambda e, tc=tc: e.reciprocal(
                smalls[:, 31:32], gvmax[:, tc * 16:tc * 16 + 1])))
            emit("vector", (lambda e, tc=tc: e.tensor_scalar(
                gvmax[:, tc * 16 + 1:tc * 16 + 2], smalls[:, 31:32], 127.0, None,
                A.mult)), inc=cV)
        c3_cV = cV.v

        for half in range(2):
            phase_dA, phase_cV = dA.v, cV.v
            for i in range(8):
                slots = []
                for s4 in range(4):
                    t_ = i * 4 + s4
                    nb = t_ % 4
                    nv = nfb[nb]
                    def ld(e, nv=nv, half=half, i=i, s4=s4):
                        return e.dma_start(
                            nv[:, :],
                            ow[half * 1024 + i * 128:half * 1024 + (i + 1) * 128,
                               s4 * 2048:(s4 + 1) * 2048])
                    w_ = [(cV, natfb_cons[nb])]
                    if i == 0:
                        w_ += [(dA, phase_dA), (cV, phase_cV)]
                    emit("sync", ld, waits=w_, inc=dS, by=16)
                    slots.append((nb, dS.v))
                for s4 in range(4):
                    t_ = i * 4 + s4
                    nb, n_dS = slots[s4]
                    qb = t_ % 2
                    qv_ = qfb[qb]
                    q_cV = quantize_nat(nfb[nb], qv_, 5,
                                        [(dS, n_dS), (cA, g3b_dA),
                                         (cP, qnfb_cons[qb]),
                                         (cP, cP.v if t_ < 2 and half == 0 else 0)])
                    natfb_cons[nb] = q_cV
                    for dc2 in range(2):
                        bk = st["trb"]
                        st["trb"] ^= 1
                        pb = TRH["t"][:, bk * 1024:(bk + 1) * 1024]
                        def tr(e, pb=pb, dc2=dc2, qv_=qv_):
                            r = None
                            for j in range(8):
                                dc = dc2 * 8 + j
                                r = e.transpose(pb[:, j * 128:(j + 1) * 128],
                                                qv_[:, dc * 128:(dc + 1) * 128],
                                                idn[:, :])
                            return r
                        emit("tensor", tr, waits=[(cV, q_cV),
                                                  (cA, st["trb_cons"][bk])], inc=cP)
                        tr_cP = cP.v
                        def ev(e, pb=pb, dc2=dc2, s4=s4, i=i):
                            hcb = s4 * 16 + dc2 * 8
                            dst = w3T[:, :].rearrange("p (n c) -> p n c", c=1024)[
                                :, hcb:hcb + 8, i * 128:(i + 1) * 128]
                            return e.copy(dst, pb[:, :].rearrange(
                                "p (n c) -> p n c", n=8))
                        emit("scalar", ev, waits=[(cP, tr_cP)], inc=cA)
                        st["trb_cons"][bk] = cA.v
                    qnfb_cons[qb] = cP.v
            w3_dA = cA.v
            emit("sync", (lambda e, half=half: e.dma_start(
                brow[0:1, 0:1024], obias[0:1, half * 1024:(half + 1) * 1024])),
                waits=[(cP, cP.v)], inc=dS, by=16)
            b3_dS = dS.v
            def bmm3(e):
                e.matmul(psX[:, 0:512], oner[:, :], brow[0:1, 0:512],
                         start=True, stop=True)
                return e.matmul(psX[:, 512:1024], oner[:, :], brow[0:1, 512:1024],
                                start=True, stop=True)
            emit("tensor", bmm3, waits=[(dS, b3_dS), (cA, cA.v)], inc=cP)
            bmm3_cP = cP.v
            emit("scalar", lambda e: e.copy(b3bc[:, 0:512], psX[:, 0:512]),
                 waits=[(cP, bmm3_cP)])
            emit("scalar", lambda e: e.copy(b3bc[:, 512:1024], psX[:, 512:1024]),
                 inc=cA)
            b3_dA = cA.v

            gv_ld = {}
            gidx = [0]

            def emit_gvload():
                idx = gidx[0]
                gidx[0] += 1
                tc_, h8 = idx // 8, idx % 8
                if tc_ >= 16:
                    return
                slot = idx % 4
                w_ = [(cV, gvrd_cons[slot]), (dA, gv_spill[(15, tc_)])]
                if idx < 4:
                    w_.append((cA, w3_dA))
                def ld(e, slot=slot, tc_=tc_, h8=h8):
                    return e.dma_start(
                        gvrd[:, slot * 1024:(slot + 1) * 1024],
                        gv_dram[tc_, :, h8 * 1024:(h8 + 1) * 1024])
                emit("sync", ld, waits=w_, inc=dS, by=16)
                gv_ld[idx] = dS.v

            x1_ld = {}

            def emit_x1load(tc_, half=half):
                if tc_ >= 16:
                    return
                b_ = tc_ % 2
                def ldx(e, tc_=tc_, half=half, b_=b_):
                    return e.dma_start(
                        x1rd[:, b_ * 1024:(b_ + 1) * 1024],
                        x1_dram[tc_, :, half * 1024:(half + 1) * 1024])
                emit("sync", ldx, waits=[(cV, x1rd_cons[b_]),
                                         (dA, x1_spill[tc_])], inc=dS, by=16)
                x1_ld[tc_] = dS.v

            for _ in range(4):
                emit_gvload()
            emit_x1load(0)
            emit_x1load(1)
            for tc in range(16):
                b = tc % 2
                n2_dA = 0
                for s4 in range(4):
                    iA = tc * 8 + s4 * 2
                    iB = iA + 1
                    gvA = gvrd[:, (iA % 4) * 1024:(iA % 4 + 1) * 1024]
                    gvB = gvrd[:, (iB % 4) * 1024:(iB % 4 + 1) * 1024]
                    emit("vector", (lambda e, gvA=gvA, tc=tc: e.tensor_scalar(
                        trash[:, 0:1024], gvA[:, :],
                        gvmax[:, tc * 16 + 1:tc * 16 + 2], None, A.mult)),
                        waits=[(dS, gv_ld[iB]), (cV, c3_cV), (cP, n2T_cons)])
                    emit("vector", lambda e: e.tensor_scalar(
                        n2q[:, 0:1024], trash[:, 0:1024], MAGIC, -MAGIC,
                        A.add, A.add))
                    emit("vector", (lambda e, gvB=gvB, tc=tc: e.tensor_scalar(
                        trash[:, 0:1024], gvB[:, :],
                        gvmax[:, tc * 16 + 1:tc * 16 + 2], None, A.mult)),
                        waits=[(dS, gv_ld.get(iB + 1, gv_ld[iB]))])
                    emit("vector", lambda e: e.tensor_scalar(
                        n2q[:, 1024:2048], trash[:, 0:1024], MAGIC, -MAGIC,
                        A.add, A.add), inc=cV)
                    q_cV = cV.v
                    gvrd_cons[iA % 4] = q_cV
                    gvrd_cons[iB % 4] = q_cV
                    emit_gvload()
                    emit_gvload()
                    for dc2 in range(2):
                        bk = st["trb"]
                        st["trb"] ^= 1
                        pb = TRH["t"][:, bk * 1024:(bk + 1) * 1024]
                        def tr(e, pb=pb, dc2=dc2):
                            r = None
                            for j in range(8):
                                dc = dc2 * 8 + j
                                r = e.transpose(pb[:, j * 128:(j + 1) * 128],
                                                n2q[:, dc * 128:(dc + 1) * 128],
                                                idn[:, :])
                            return r
                        emit("tensor", tr, waits=[(cV, q_cV),
                                                  (cA, st["trb_cons"][bk])], inc=cP)
                        tr_cP = cP.v
                        def ev(e, pb=pb, dc2=dc2, s4=s4):
                            hcb = s4 * 16 + dc2 * 8
                            dst = n2T[:, hcb * 128:(hcb + 8) * 128]
                            return e.copy(dst, pb[:, :])
                        emit("scalar", ev, waits=[(cP, tr_cP)], inc=cA)
                        st["trb_cons"][bk] = cA.v
                    n2_dA = cA.v
                xr = x1rd[:, b * 1024:(b + 1) * 1024]
                for jw in range(2):
                    bank = psA[:, (b * 2 + jw) * 512:(b * 2 + jw + 1) * 512]
                    def mm3(e, jw=jw, bank=bank):
                        r = None
                        for hc in range(64):
                            r = e.matmul(bank,
                                         n2T[:, hc * 128:(hc + 1) * 128],
                                         w3T[:, hc * 1024 + jw * 512:][:, 0:512],
                                         start=(hc == 0), stop=(hc == 63))
                        return r
                    w = [(cA, n2_dA), (cA, w3_dA)]
                    pc = psF_cons.get(b * 2 + jw)
                    if pc is not None:
                        w.append(pc)
                    emit("tensor", mm3, waits=w, inc=cP)
                    mm3_cP = cP.v
                    if jw == 1:
                        n2T_cons = mm3_cP
                    ob_ = outbuf[:, b * 1024 + jw * 512:b * 1024 + (jw + 1) * 512]
                    emit("vector", (lambda e, bank=bank, tc=tc, jw=jw, ob_=ob_:
                                    e.scalar_tensor_tensor(
                                        ob_[:, :], bank, c3_sc[:, tc:tc + 1],
                                        b3bc[:, jw * 512:(jw + 1) * 512],
                                        A.mult, A.add)),
                         waits=[(cP, mm3_cP), (cA, b3_dA),
                                (dA, max(outb_cons[0], outb_cons[1]))])
                    emit("vector", (lambda e, ob_=ob_, xr=xr, jw=jw: e.tensor_add(
                        ob_[:, :], ob_[:, :], xr[:, jw * 512:(jw + 1) * 512])),
                        waits=[(dS, x1_ld.get(tc + 1, x1_ld[tc]))], inc=cV)
                    psF_cons[b * 2 + jw] = (cV, cV.v)
                ev_cV = cV.v
                x1rd_cons[b] = ev_cV
                emit_x1load(tc + 2)
                def sto(e, tc=tc, half=half, b=b):
                    return e.dma_start(
                        out_t[tc * 128:(tc + 1) * 128,
                              half * 1024:(half + 1) * 1024],
                        outbuf[:, b * 1024:(b + 1) * 1024])
                emit("scalar", sto, waits=[(cV, ev_cV)], inc=dA, by=16)
                outb_cons[b] = dA.v

    emit("sync", lambda e: e.nop(),
         waits=[(dA, dA.v), (dS, dS.v), (dG, dG.v), (cP, cP.v), (cA, cA.v),
                (cV, cV.v), (cG, cG.v)])

    return B.finish()


_ROPE_CACHE = {}


def _rope_tables():
    if "k" in _ROPE_CACHE:
        return _ROPE_CACHE["k"]
    d = 128
    inv_freq = (np.float32(1.0) /
                (np.float32(10000.0) **
                 (np.arange(0, d, 2, dtype=np.float32) / np.float32(d))))
    ang = np.arange(S, dtype=np.float32)[:, None] * inv_freq[None, :]
    cos = np.cos(ang).astype(np.float32)
    sin = np.sin(ang).astype(np.float32)
    C = np.repeat(cos.T, 2, axis=0)
    Sg = np.empty((128, S), np.float32)
    Sg[0::2] = -sin.T
    Sg[1::2] = sin.T
    _ROPE_CACHE["k"] = (np.ascontiguousarray(C), np.ascontiguousarray(Sg))
    return _ROPE_CACHE["k"]


def kernel(L0_attn_x, L0_attn_w_kv_down, L0_attn_w_kv_up_k, L0_attn_w_kv_up_v,
           L0_attn_w_q_down, L0_attn_w_q_up, L0_attn_w_k_rope, L0_attn_w_q_rope,
           L0_attn_w_out, L0_ffn_gate_w, L0_ffn_gate_b, L0_ffn_val_w,
           L0_ffn_val_b, L0_ffn_out_w, L0_ffn_out_b):
    x = np.asarray(L0_attn_x, dtype=np.float32)
    C, Sg = _rope_tables()
    common = {
        "w_kvd": np.asarray(L0_attn_w_kv_down, np.float32),
        "w_kuk": np.asarray(L0_attn_w_kv_up_k, np.float32),
        "w_kuv": np.asarray(L0_attn_w_kv_up_v, np.float32),
        "w_qd": np.asarray(L0_attn_w_q_down, np.float32),
        "w_qu": np.asarray(L0_attn_w_q_up, np.float32),
        "w_kr": np.asarray(L0_attn_w_k_rope, np.float32),
        "w_qr": np.asarray(L0_attn_w_q_rope, np.float32),
        "w_out": np.asarray(L0_attn_w_out, np.float32),
        "gw": np.asarray(L0_ffn_gate_w, np.float32),
        "gbias": np.asarray(L0_ffn_gate_b, np.float32).reshape(1, HF),
        "vw": np.asarray(L0_ffn_val_w, np.float32),
        "vbias": np.asarray(L0_ffn_val_b, np.float32).reshape(1, HF),
        "ow": np.asarray(L0_ffn_out_w, np.float32),
        "obias": np.asarray(L0_ffn_out_b, np.float32).reshape(1, D),
        "ropec": C, "ropes": Sg,
        "ident": np.eye(128, dtype=bfloat16),
        "ones_col": np.ones((128, 1), np.float32),
        "ones_row": np.ones((1, 128), np.float32),
    }
    in_maps = []
    for c in range(8):
        be, qi = c // 4, c % 4
        q0 = qi * TQ
        m = dict(common)
        m["x_be"] = x[be]
        m["x_q"] = x[be, q0:q0 + TQ]
        m["ropeqc"] = np.ascontiguousarray(C[:, q0:q0 + TQ])
        m["ropeqs"] = np.ascontiguousarray(Sg[:, q0:q0 + TQ])
        in_maps.append(m)
    nc = build_program()
    res = run_bass_kernel_spmd(nc, in_maps, core_ids=list(range(8)))
    out = np.empty((2, S, D), np.float32)
    for c in range(8):
        be, qi = c // 4, c % 4
        out[be, qi * TQ:(qi + 1) * TQ] = res.results[c]["out"]
    return out


# version 08
# speedup vs baseline: 1.0004x; 1.0004x over previous
import numpy as np
import ml_dtypes
import concourse.bass as bass
import concourse.mybir as mybir
from concourse.bass_utils import run_bass_kernel_spmd

F32 = mybir.dt.float32
BF16 = mybir.dt.bfloat16
A = mybir.AluOpType
AF = mybir.ActivationFunctionType
AX = mybir.AxisListType.X
bfloat16 = ml_dtypes.bfloat16

D = 2048
S = 8192
TQ = 2048
HEADS = 16
KVC = 512
QC = 1024
HF = 8192
NKC = S // 128
MAGIC = 12582912.0
ATTN_ONLY = False
TRUNC = None
AB_N = 3
NO_GAMMA = False
NO_OTSPILL = False

ENGS = ("sync", "tensor", "vector", "scalar", "gpsimd")
SWAPMASK = []
for _i in range(16):
    SWAPMASK += [2 * _i + 1, 2 * _i]
KB = 1024


class Ctr:
    def __init__(self, name):
        self.name = name
        self.v = 0
        self.h = None


class Builder:
    def __init__(self):
        self.nc = bass.Bass("TRN2", target_bir_lowering=False,
                    detect_race_conditions=False)
        self.ops = {e: [] for e in ENGS}
        self.lastw = {}
        self.dS, self.dA, self.dG = Ctr("dS"), Ctr("dA"), Ctr("dG")
        self.cP, self.cA, self.cV, self.cG = Ctr("cP"), Ctr("cA"), Ctr("cV"), Ctr("cG")
        self.ctrs = [self.dS, self.dA, self.dG, self.cP, self.cA, self.cV, self.cG]

    def emit(self, eng, fn, waits=(), inc=None, by=1):
        ws = []
        for c, v in waits:
            if v <= 0:
                continue
            k = (eng, c.name)
            if self.lastw.get(k, -1) >= v:
                continue
            self.lastw[k] = v
            ws.append((c, v))
        self.ops[eng].append((ws, fn, inc, by))
        if inc is not None:
            inc.v += by
            return inc.v
        return None

    def fence(self, eng):
        self.emit(eng, lambda e: e.nop(), waits=[(c, c.v) for c in self.ctrs])

    def fence_all(self):
        for eng in ENGS:
            self.fence(eng)

    def finish(self):
        nc = self.nc
        guards = []
        for c in self.ctrs:
            g = nc.semaphore(c.name)
            c.h = g.__enter__()
            guards.append(g)
        with nc.Block() as block:
            def mk(eng_name):
                need_drain = eng_name in ("vector", "gpsimd", "scalar")
                always_drain = eng_name == "vector"
                def f(eng):
                    for ws, fn, inc, by in self.ops[eng_name]:
                        for c, v in ws:
                            eng.wait_ge(c.h, v)
                        r = fn(eng)
                        if inc is not None and (by == 16 or not need_drain):
                            r.then_inc(inc.h, by)
                        elif inc is not None:
                            eng.drain().then_inc(inc.h, by)
                        elif always_drain and r is not None:
                            eng.drain()
                return f
            block.sync(mk("sync"))
            block.tensor(mk("tensor"))
            block.vector(mk("vector"))
            block.scalar(mk("scalar"))
            block.gpsimd(mk("gpsimd"))
        for g in guards:
            g.__exit__(None, None, None)
        return nc


def build_program():
    B = Builder()
    nc = B.nc
    dS, dA, dG = B.dS, B.dA, B.dG
    cP, cA, cV, cG = B.cP, B.cA, B.cV, B.cG
    emit = B.emit

    def din(name, shape, dtype=F32):
        return nc.dram_tensor(name, list(shape), dtype, kind="ExternalInput")

    x_be = din("x_be", [S, D])
    x_q = din("x_q", [TQ, D])
    w_kvd = din("w_kvd", [D, KVC])
    w_kuk = din("w_kuk", [KVC, D])
    w_kuv = din("w_kuv", [KVC, D])
    w_qd = din("w_qd", [D, QC])
    w_qu = din("w_qu", [QC, D])
    w_kr = din("w_kr", [D, 128])
    w_qr = din("w_qr", [D, 128])
    w_out = din("w_out", [D, D])
    gw = din("gw", [HF, D])
    gbias = din("gbias", [1, HF])
    vw = din("vw", [HF, D])
    vbias = din("vbias", [1, HF])
    ow = din("ow", [D, HF])
    obias = din("obias", [1, D])
    ropec = din("ropec", [128, S])
    ropes = din("ropes", [128, S])
    ropeqc = din("ropeqc", [128, TQ])
    ropeqs = din("ropeqs", [128, TQ])
    ident = din("ident", [128, 128], BF16)
    ones_col = din("ones_col", [128, 1])
    ones_row = din("ones_row", [1, 128])

    out_t = nc.dram_tensor("out", [TQ, D], F32, kind="ExternalOutput")
    x1_dram = nc.dram_tensor("x1_scratch", [16, 128, D], F32,
                             kind="ExternalOutput" if TRUNC == "p2" else "Internal")
    gv_dram = nc.dram_tensor("gv_scratch", [16, 128, HF], F32,
                             kind="ExternalOutput" if TRUNC in ("f1", "f1w") else "Internal")
    ot_dram = nc.dram_tensor("ot_scratch", [HEADS, 128, TQ], BF16,
                             kind="ExternalOutput" if TRUNC == "p1" else "Internal")
    if TRUNC == "f1w":
        dbg_gt = nc.dram_tensor("dbg_gt", [16, 128, 8192], BF16, kind="ExternalOutput")
        dbg_vt = nc.dram_tensor("dbg_vt", [16, 128, 8192], BF16, kind="ExternalOutput")
        dbg_g3 = nc.dram_tensor("dbg_g3", [128, 16], F32, kind="ExternalOutput")
    if TRUNC == "h":
        dbg_h = nc.dram_tensor("dbg_h", [128, 2 * 8192], BF16, kind="ExternalOutput")
        dbg_sm = nc.dram_tensor("dbg_sm", [128, 32], F32, kind="ExternalOutput")
    if TRUNC == "p1ab":
        dbg_kt = nc.dram_tensor("dbg_kt", [128, S], BF16, kind="ExternalOutput")
        dbg_v = nc.dram_tensor("dbg_v", [128, S], BF16, kind="ExternalOutput")
        dbg_qt = nc.dram_tensor("dbg_qt", [128, TQ], BF16, kind="ExternalOutput")
    if TRUNC == "p0q":
        dbg_kv = nc.dram_tensor("dbg_kv", [128, 4 * S], BF16, kind="ExternalOutput")
        dbg_kr = nc.dram_tensor("dbg_kr", [128, S], BF16, kind="ExternalOutput")
        dbg_q = nc.dram_tensor("dbg_q", [128, 8 * TQ], BF16, kind="ExternalOutput")
        dbg_qr = nc.dram_tensor("dbg_qr", [128, TQ], BF16, kind="ExternalOutput")

    idn = nc.alloc_sbuf_tensor("idn", [128, 128], BF16)
    onec = nc.alloc_sbuf_tensor("onec", [128, 1], F32)
    oner = nc.alloc_sbuf_tensor("oner", [1, 128], F32)
    g3 = nc.alloc_sbuf_tensor("g3", [1, 8], F32)
    g3b = nc.alloc_sbuf_tensor("g3b", [128, 16], F32)
    gammax = nc.alloc_sbuf_tensor("gammax", [128, 16], F32)
    cg_sc = nc.alloc_sbuf_tensor("cg_sc", [128, 16], F32)
    cv_sc = nc.alloc_sbuf_tensor("cv_sc", [128, 16], F32)
    c3_sc = nc.alloc_sbuf_tensor("c3_sc", [128, 16], F32)
    gvmax = nc.alloc_sbuf_tensor("gvmax", [128, 256], F32)
    gacc = nc.alloc_sbuf_tensor("gacc", [128, 4], F32)
    trash = nc.alloc_sbuf_tensor("trash", [128, 1024], F32)
    brow = nc.alloc_sbuf_tensor("brow", [1, 1024], F32)

    arena_bytes = nc.sbuf_bytes_remaining // 32 * 32 - 32
    arena = nc.alloc_sbuf_tensor("ARENA", [128, arena_bytes], mybir.dt.uint8)
    abase = nc.lookup_mloc(arena).addr
    assert arena_bytes >= 196 * KB, f"arena too small: {arena_bytes}"

    def place(name, off, shape, dtype):
        per_part = int(np.prod(shape[1:])) * mybir.dt.size(dtype)
        assert off + per_part <= arena_bytes, (name, off, per_part, arena_bytes)
        return nc.alloc_sbuf_tensor_at(name, list(shape), dtype, offset=abase + off)

    smalls = place("smalls", 196 * KB, [128, 32], F32)
    kv_latT = place("kv_latT", 0, [128, 4 * S], BF16)
    k_rT = place("k_rT", 64 * KB, [128, S], BF16)
    q_latT = place("q_latT", 80 * KB, [128, 8 * TQ], BF16)
    qt_r = place("qt_r", 112 * KB, [128, TQ], BF16)
    hTq = place("hTq", 0, [128, 2 * 8192], BF16)
    ropebq = place("ropebq", 32 * KB, [128, 1024], F32)
    xbq = place("xbq", 36 * KB, [128, 2 * D], F32)
    hbfq = place("hbfq", 52 * KB, [128, 2 * D], BF16)
    qd_sb = place("qd", 116 * KB, [128, 16 * QC], BF16)
    qr_sb = place("qr", 148 * KB, [128, 16 * 128], BF16)
    kvd_sb = place("kvd", 116 * KB, [128, 16 * KVC], BF16)
    kr_sb = place("kr", 132 * KB, [128, 16 * 128], BF16)
    xb0 = place("xb0", 136 * KB, [128, 2 * D], F32)
    hbf0 = place("hbf0", 152 * KB, [128, 2 * D], BF16)
    hT0 = place("hT0", 160 * KB, [128, 2 * 8192], BF16)
    ropeb0 = place("ropeb0", 192 * KB, [128, 1024], F32)
    kuk_h = place("kuk_h", 116 * KB, [128, 2 * 512], BF16)
    kuv_h = place("kuv_h", 118 * KB, [128, 2 * 512], BF16)
    qu_h = place("qu_h", 120 * KB, [128, 2 * 1024], BF16)
    ktc = place("ktc", 124 * KB, [128, S], BF16)
    v_sb = place("v_sb", 140 * KB, [128, S], BF16)
    qtc = place("qtc", 156 * KB, [128, TQ], BF16)
    pt = place("pt", 160 * KB, [128, 2048], BF16)
    acc = place("acc", 164 * KB, [128, 1024], F32)
    inv_row = place("inv_row", 168 * KB, [1, 1024], F32)
    inv_bc = place("inv_bc", 172 * KB, [128, 1024], F32)
    osb = place("osb", 176 * KB, [128, 1024], BF16)
    gstage = place("gstage", 180 * KB, [128, 2 * 2048], F32)
    wout_sb = place("wout", 0, [128, 16 * D], BF16)
    otqc = place("otqc", 64 * KB, [128, 2 * 2048], BF16)
    xq_sb = place("xq_sb", 72 * KB, [128, 2 * D], F32)
    x1t = place("x1t", 88 * KB, [128, D], F32)
    h2t = place("h2t", 96 * KB, [128, D], F32)
    ntok = place("ntok", 104 * KB, [128, D], BF16)
    nT = place("nT", 108 * KB, [128, 16 * TQ], BF16)
    gwT = place("gwT", 0, [128, 2 * 16 * 512], BF16)
    vwT = place("vwT", 32 * KB, [128, 2 * 16 * 512], BF16)
    natst = place("natst", 64 * KB, [128, 2 * D], F32)
    qnat = place("qnat", 80 * KB, [128, 2 * D], BF16)
    natst2 = place("natst2", 172 * KB, [128, 2 * D], F32)
    qnat2 = place("qnat2", 188 * KB, [128, 2 * D], BF16)
    biasb = place("biasb", 88 * KB, [128, 2 * 1024], F32)
    gvt = place("gvt", 96 * KB, [128, 4 * 512], F32)
    w3T = place("w3T", 0, [128, 64 * 1024], BF16)
    b3bc = place("b3bc", 128 * KB, [128, 1024], F32)
    natfb = place("natfb", 132 * KB, [128, 2 * D], F32)
    natfb2 = place("natfb2", 152 * KB, [128, 2 * D], F32)
    x1rd = place("x1rd", 132 * KB, [128, 2 * 1024], F32)
    outbuf = place("outbuf", 140 * KB, [128, 2 * 1024], F32)
    qnfb = place("qnfb", 148 * KB, [128, D], BF16)
    qnfb2 = place("qnfb2", 188 * KB, [128, D], BF16)
    gvrd = place("gvrd", 152 * KB, [128, 2 * D], F32)
    n2q = place("n2q", 168 * KB, [128, D], BF16)
    n2T = place("n2T", 172 * KB, [128, 64 * 128], BF16)

    psA_g = nc.psum_tensor("psA", [128, 2048], F32)
    psA = psA_g.__enter__()
    psTR_g = nc.psum_tensor("psTR", [128, 2048], BF16)
    psTR = psTR_g.__enter__()
    TRH = {"t": psTR}
    psX0_g = nc.psum_tensor("psX0", [128, 1024], F32)
    SQD = {"t": psX0_g.__enter__()}

    emit("sync", lambda e: e.dma_start(idn[:, :], ident[:, :]), inc=dS, by=16)
    emit("sync", lambda e: e.dma_start(onec[:, :], ones_col[:, :]), inc=dS, by=16)
    emit("sync", lambda e: e.dma_start(oner[:, :], ones_row[:, :]), inc=dS, by=16)
    consts_dS = dS.v
    emit("gpsimd", lambda e: e.memset(gacc[:, :], 0.0), inc=cG)

    def load_chunked(dst, src, R, coloff, C, extra_waits=()):
        n = R // 128
        view = src[:, coloff:coloff + C].rearrange("(n p) c -> p n c", p=128)
        def f(e, dst=dst, view=view, n=n, C=C):
            return e.dma_start(dst[:, 0:n * C].rearrange("p (n c) -> p n c", n=n), view)
        return emit("gpsimd", f, waits=extra_waits, inc=dG, by=16)

    st = {
        "xb_cons": [0, 0],
        "xb_consA": [0, 0],
        "hbf_cons": [0, 0],
        "trb": 0,
        "trb_cons": [0, 0],
        "wq": 0,
        "psA_cons": {},
        "rope_cons": 0,
        "osb_dma": 0,
        "oevac": [0, 0],
    }

    def ln_tile(xv, hv, par, extra_waits):
        c0 = par * 12
        dmp = SQD["t"]
        emit("scalar", (lambda e, xv=xv, c0=c0, dmp=dmp: e.activation(
            dmp[:, :], xv[:, 0:1024], AF.Square,
            accum_out=smalls[:, c0 + 1:c0 + 2])), waits=extra_waits)
        emit("scalar", (lambda e, xv=xv, c0=c0, dmp=dmp: e.activation(
            dmp[:, :], xv[:, 1024:2048], AF.Square,
            accum_out=smalls[:, c0 + 10:c0 + 11])), inc=cA)
        sq_cA = cA.v
        emit("vector", (lambda e, xv=xv, c0=c0: e.reduce_sum(
            smalls[:, c0:c0 + 1], xv[:, :], AX)), waits=extra_waits)
        emit("vector", (lambda e, c0=c0: e.tensor_add(
            smalls[:, c0 + 1:c0 + 2], smalls[:, c0 + 1:c0 + 2],
            smalls[:, c0 + 10:c0 + 11])), waits=[(cA, sq_cA)])
        emit("vector", (lambda e, c0=c0: e.tensor_scalar(
            smalls[:, c0 + 2:c0 + 3], smalls[:, c0:c0 + 1], 1.0 / D, None, A.mult)))
        emit("vector", (lambda e, c0=c0: e.tensor_mul(
            smalls[:, c0 + 3:c0 + 4], smalls[:, c0 + 2:c0 + 3],
            smalls[:, c0 + 2:c0 + 3])))
        emit("vector", (lambda e, c0=c0: e.tensor_scalar(
            smalls[:, c0 + 4:c0 + 5], smalls[:, c0 + 1:c0 + 2], 1.0 / D, None,
            A.mult)))
        emit("vector", (lambda e, c0=c0: e.tensor_sub(
            smalls[:, c0 + 5:c0 + 6], smalls[:, c0 + 4:c0 + 5],
            smalls[:, c0 + 3:c0 + 4])))
        emit("vector", (lambda e, c0=c0: e.tensor_scalar(
            smalls[:, c0 + 6:c0 + 7], smalls[:, c0 + 5:c0 + 6], 1e-5, None,
            A.add)))
        emit("vector", lambda e: e.nop(), inc=cV)
        veps_cV = cV.v
        emit("scalar", (lambda e, c0=c0: e.activation(
            smalls[:, c0 + 7:c0 + 8], smalls[:, c0 + 6:c0 + 7], AF.Ln)),
            waits=[(cV, veps_cV)])
        emit("scalar", (lambda e, c0=c0: e.activation(
            smalls[:, c0 + 8:c0 + 9], smalls[:, c0 + 7:c0 + 8], AF.Exp,
            scale=-0.5)), inc=cA)
        inv_cA = cA.v
        return c0, sq_cA, inv_cA

    def ln_norm(xv, hv, c0, inv_cA, pe_wait):
        emit("vector", (lambda e, xv=xv, hv=hv, c0=c0: e.tensor_scalar(
            hv[:, :], xv[:, :], smalls[:, c0 + 2:c0 + 3],
            smalls[:, c0 + 8:c0 + 9], A.subtract, A.mult)),
            waits=[(cA, inv_cA)] + pe_wait, inc=cV)
        return cV.v

    def transpose_tile(hv, dstH, chunk_base, dsc, col, src_cV):
        nch = int(np.prod(dstH.shape[1:])) // dsc
        trh = TRH["t"]
        for dc2 in range(2):
            bk = st["trb"]
            st["trb"] ^= 1
            pb = trh[:, bk * 1024:(bk + 1) * 1024]
            def tr(e, hv=hv, pb=pb, dc2=dc2):
                r = None
                for j in range(8):
                    dc = dc2 * 8 + j
                    r = e.transpose(pb[:, j * 128:(j + 1) * 128],
                                    hv[:, dc * 128:(dc + 1) * 128], idn[:, :])
                return r
            emit("tensor", tr,
                 waits=[(cV, src_cV), (cA, st["trb_cons"][bk]), (dS, consts_dS)],
                 inc=cP)
            tr_cP = cP.v
            def ev(e, dstH=dstH, pb=pb, dc2=dc2, col=col, dsc=dsc, nch=nch,
                   chunk_base=chunk_base):
                cb = chunk_base + dc2 * 8
                dst = dstH[:, :].rearrange("p (n c) -> p n c", n=nch)[
                    :, cb:cb + 8, col:col + 128]
                return e.copy(dst, pb[:, :].rearrange("p (n c) -> p n c", n=8))
            emit("scalar", ev, waits=[(cP, tr_cP)], inc=cA)
            st["trb_cons"][bk] = cA.v
        return cA.v

    def emit_rope(bank, ropeb, dst, tw, mr_cP, rope_dS):
        t1 = trash[:, 0:512]
        t2 = trash[:, 512:1024]
        emit("vector", (lambda e, bank=bank, t1=t1: e.stream_shuffle(
            t1[:, :], bank[:, :], SWAPMASK)),
            waits=[(cP, mr_cP), (dS, rope_dS)], inc=cV)
        emit("vector", (lambda e, t1=t1, ropeb=ropeb: e.tensor_mul(
            t1[:, :], t1[:, :], ropeb[:, 512:1024])))
        emit("vector", (lambda e, bank=bank, t2=t2, ropeb=ropeb: e.tensor_mul(
            t2[:, :], bank[:, :], ropeb[:, 0:512])), inc=cV)
        emit("vector", (lambda e, dst=dst, tw=tw, t1=t1, t2=t2: e.tensor_add(
            dst[:, tw * 512:(tw + 1) * 512], t2[:, :], t1[:, :])), inc=cV)
        st["rope_cons"] = cV.v

    def p0_like(nt, x_src, per_window, rc_src, rs_src, xbuf, hbuf, hTh, ropeb,
                first_wait=()):
        first = [True]
        for tw in range(nt):
            b = tw % 2
            hTv = hTh[:, b * 8192:(b + 1) * 8192]
            emit("sync", (lambda e, tw=tw, rc_src=rc_src, ropeb=ropeb: e.dma_start(
                ropeb[:, 0:512], rc_src[:, tw * 512:(tw + 1) * 512])),
                waits=[(cV, st["rope_cons"])], inc=dS, by=16)
            emit("sync", (lambda e, tw=tw, rs_src=rs_src, ropeb=ropeb: e.dma_start(
                ropeb[:, 512:1024], rs_src[:, tw * 512:(tw + 1) * 512])),
                inc=dS, by=16)
            rope_dS = dS.v
            for t in range(4):
                g = tw * 4 + t
                gb_ = g % 2
                xv = xbuf[:, gb_ * D:(gb_ + 1) * D]
                hv = hbuf[:, gb_ * D:(gb_ + 1) * D]
                fw = list(first_wait) if first[0] else []
                first[0] = False
                emit("sync", (lambda e, xv=xv, g=g, x_src=x_src: e.dma_start(
                    xv[:, :], x_src[g * 128:(g + 1) * 128, :])),
                    waits=[(cV, st["xb_cons"][gb_]),
                           (cA, st["xb_consA"][gb_])] + fw, inc=dS, by=16)
                x_dS = dS.v
                c0, sq_cA, inv_cA = ln_tile(xv, hv, gb_, [(dS, x_dS)])
                st["xb_consA"][gb_] = sq_cA
                norm_cV = ln_norm(xv, hv, c0, inv_cA,
                                  [(cP, st["hbf_cons"][gb_])])
                st["xb_cons"][gb_] = norm_cV
                transpose_tile(hv, hTh, b * 16, 512, t * 128, norm_cV)
                st["hbf_cons"][gb_] = cP.v
            hT_dA = cA.v
            per_window(tw, hTv, hT_dA, rope_dS, ropeb)

    def proj_window(tw, hTv, hT_dA, rope_dS, ropeb, wsb, wC, nlat, dstT, dst_S,
                    wd, rope_wsb, rope_dst):
        for lc in range(nlat):
            bank = psA[:, (lc % 2) * 512:(lc % 2 + 1) * 512]
            def mm(e, hTv=hTv, lc=lc, bank=bank, wsb=wsb, wC=wC):
                r = None
                for dc in range(16):
                    r = e.matmul(bank[:, :],
                                 wsb[:, dc * wC + lc * 128:][:, 0:128],
                                 hTv[:, dc * 512:(dc + 1) * 512],
                                 start=(dc == 0), stop=(dc == 15))
                return r
            w = [(cA, hT_dA), (dG, wd)]
            pc = st["psA_cons"].get(lc % 2)
            if pc is not None:
                w.append(pc)
            emit("tensor", mm, waits=w, inc=cP)
            mm_cP = cP.v
            emit("vector", (lambda e, lc=lc, tw=tw, bank=bank, dstT=dstT,
                            dst_S=dst_S: e.tensor_copy(
                                dstT[:, lc * dst_S + tw * 512:][:, 0:512], bank[:, :])),
                 waits=[(cP, mm_cP)], inc=cV)
            st["psA_cons"][lc % 2] = (cV, cV.v)
        bank = psA[:, 1024:1536]
        def mmr(e, hTv=hTv, bank=bank, rope_wsb=rope_wsb):
            r = None
            for dc in range(16):
                r = e.matmul(bank[:, :], rope_wsb[:, dc * 128:(dc + 1) * 128],
                             hTv[:, dc * 512:(dc + 1) * 512],
                             start=(dc == 0), stop=(dc == 15))
            return r
        w = [(cA, hT_dA), (dG, wd)]
        pc = st["psA_cons"].get(2)
        if pc is not None:
            w.append(pc)
        emit("tensor", mmr, waits=w, inc=cP)
        emit_rope(bank, ropeb, rope_dst, tw, cP.v, rope_dS)
        st["psA_cons"][2] = (cV, cV.v)

    load_chunked(qd_sb, w_qd, D, 0, QC)
    w0q_d = load_chunked(qr_sb, w_qr, D, 0, 128)
    p0_like(4, x_q,
            lambda tw, hTv, hT_dA, rope_dS, ropeb: proj_window(
                tw, hTv, hT_dA, rope_dS, ropeb, qd_sb, QC, 8, q_latT, TQ,
                w0q_d, qr_sb, qt_r),
            ropeqc, ropeqs, xbq, hbfq, hTq, ropebq)

    load_chunked(kvd_sb, w_kvd, D, 0, KVC, extra_waits=[(cP, cP.v)])
    w0_d = load_chunked(kr_sb, w_kr, D, 0, 128)
    st["xb_cons"] = [0, 0]
    st["xb_consA"] = [0, 0]
    st["hbf_cons"] = [0, 0]
    st["rope_cons"] = cV.v
    p0_like(16, x_be,
            lambda tw, hTv, hT_dA, rope_dS, ropeb: proj_window(
                tw, hTv, hT_dA, rope_dS, ropeb, kvd_sb, KVC, 4, kv_latT, S,
                w0_d, kr_sb, k_rT),
            ropec, ropes, xb0, hbf0, hT0, ropeb0,
            first_wait=[(cP, cP.v)])

    if TRUNC == "h":
        emit("scalar", lambda e: e.dma_start(dbg_h[:, :], hT0[:, :]),
             waits=[(cV, cV.v), (cA, cA.v), (cP, cP.v)], inc=dA, by=16)
        emit("scalar", lambda e: e.dma_start(dbg_sm[:, :], smalls[:, :]), inc=dA, by=16)
        emit("sync", lambda e: e.nop(),
             waits=[(dA, dA.v), (dS, dS.v), (dG, dG.v), (cP, cP.v), (cA, cA.v),
                    (cV, cV.v), (cG, cG.v)])
        return B.finish()

    def finish_trunc():
        emit("sync", lambda e: e.nop(),
             waits=[(dA, dA.v), (dS, dS.v), (dG, dG.v), (cP, cP.v), (cA, cA.v),
                    (cV, cV.v), (cG, cG.v)])
        return B.finish()

    if TRUNC == "p0q":
        emit("scalar", lambda e: e.dma_start(dbg_kv[:, :], kv_latT[:, :]),
             waits=[(cV, cV.v), (cA, cA.v), (cP, cP.v)], inc=dA, by=16)
        emit("scalar", lambda e: e.dma_start(dbg_kr[:, :], k_rT[:, :]), inc=dA, by=16)
        emit("scalar", lambda e: e.dma_start(dbg_q[:, :], q_latT[:, :]), inc=dA, by=16)
        emit("scalar", lambda e: e.dma_start(dbg_qr[:, :], qt_r[:, :]), inc=dA, by=16)
        return finish_trunc()

    gsrcs = [("g", i) for i in range(64)] + [("v", i) for i in range(64)] + \
            [("o", i) for i in reversed(range(64))]
    gstate = {"idx": 0, "cons": [0, 0]}

    def emit_gamma_tiles(n):
        for _ in range(n):
            if gstate["idx"] >= len(gsrcs):
                return
            kind, ti = gsrcs[gstate["idx"]]
            b = gstate["idx"] % 2
            gstate["idx"] += 1
            gv_ = gstage[:, b * 2048:(b + 1) * 2048]
            mat = {"g": 0, "v": 1, "o": 2}[kind]
            src = {"g": gw, "v": vw, "o": ow}[kind]
            if kind != "o":
                def ld(e, gv_=gv_, src=src, ti=ti):
                    return e.dma_start(gv_[:, :], src[ti * 128:(ti + 1) * 128, :])
            else:
                def ld(e, gv_=gv_, src=src, ti=ti):
                    r, q = ti // 4, ti % 4
                    return e.dma_start(
                        gv_[:, :], src[r * 128:(r + 1) * 128, q * 2048:(q + 1) * 2048])
            w = [(cV, gstate["cons"][b])]
            if gstate["idx"] <= 2:
                w += [(cP, cP.v), (cV, cV.v), (cA, cA.v)]
            emit("sync", ld, waits=w, inc=dS, by=16)
            t_dS = dS.v
            emit("vector", (lambda e, gv_=gv_: e.tensor_reduce(
                smalls[:, 29:30], gv_[:, :], AX, A.add, apply_absolute_value=True)),
                waits=[(dS, t_dS)], inc=cV)
            emit("vector", (lambda e, mat=mat: e.tensor_add(
                gacc[:, mat:mat + 1], gacc[:, mat:mat + 1], smalls[:, 29:30])),
                inc=cV)
            gstate["cons"][b] = cV.v

    def emit_gamma_final():
        gp_cV = cV.v
        bank = psA[0:1, 0:512]
        w = [(cV, gp_cV), (dS, consts_dS)]
        pc = st["psA_cons"].get(0)
        if pc is not None:
            w.append(pc)
        emit("tensor", (lambda e, bank=bank: e.matmul(
            bank[0:1, 0:4], onec[:, :], gacc[:, 0:4], start=True, stop=True)),
            waits=w, inc=cP)
        s_cP = cP.v
        emit("vector", (lambda e, bank=bank: e.tensor_copy(
            g3[0:1, 0:4], bank[0:1, 0:4])), waits=[(cP, s_cP)], inc=cV)
        st["psA_cons"][0] = (cV, cV.v)
        emit("vector", lambda e: e.tensor_scalar(
            g3[0:1, 0:3], g3[0:1, 0:3], 1.0 / (HF * D), 1e-5, A.mult, A.max), inc=cV)
        emit("vector", lambda e: e.reciprocal(g3[0:1, 3:6], g3[0:1, 0:3]), inc=cV)
        g_cV = cV.v
        bank2 = psA[:, 512:1024]
        w = [(cV, g_cV)]
        pc = st["psA_cons"].get(1)
        if pc is not None:
            w.append(pc)
        emit("tensor", (lambda e, bank2=bank2: e.matmul(
            bank2[:, 0:6], oner[:, :], g3[0:1, 0:6], start=True, stop=True)),
            waits=w, inc=cP)
        b_cP = cP.v
        emit("scalar", (lambda e, bank2=bank2: e.copy(g3b[:, 0:6], bank2[:, 0:6])),
             waits=[(cP, b_cP)])
        emit("scalar", (lambda e, bank2=bank2: e.activation(
            g3b[:, 6:9], bank2[:, 0:3], AF.Copy, scale=0.5)))
        emit("scalar", (lambda e, bank2=bank2: e.activation(
            g3b[:, 9:12], bank2[:, 0:3], AF.Copy, scale=-0.5)), inc=cA)
        st["psA_cons"][1] = (cA, cA.v)
        return cA.v

    SQD["t"] = None
    psX0_g.__exit__(None, None, None)
    psTR_g.__exit__(None, None, None)
    psO_g = nc.psum_tensor("psO", [128, 2048], F32)
    ps_O = psO_g.__enter__()

    wslice_d = [0, 0]

    def load_head_slices(h):
        b = h % 2
        def mkld(wdram, dstbuf, n):
            def f(e, wdram=wdram, dstbuf=dstbuf, n=n, h=h):
                view = wdram[:, h * 128:(h + 1) * 128].rearrange(
                    "(n p) c -> p n c", p=128)
                return e.dma_start(dstbuf.rearrange("p (n c) -> p n c", n=n), view)
            return f
        w = [(cP, cP.v)]
        emit("gpsimd", mkld(w_kuk, kuk_h[:, b * 512:(b + 1) * 512], 4),
             waits=w, inc=dG, by=16)
        emit("gpsimd", mkld(w_kuv, kuv_h[:, b * 512:(b + 1) * 512], 4), inc=dG, by=16)
        emit("gpsimd", mkld(w_qu, qu_h[:, b * 1024:(b + 1) * 1024], 8), inc=dG, by=16)
        wslice_d[b] = dG.v

    if not (TRUNC == "p1ab" and AB_N < 0):
        load_head_slices(0)
    exp_c = {}
    sv_c = {}
    sg_c = {}
    iter_c = {}
    psO_cons = {}

    B.fence_all()

    for h in range(HEADS):
        b = h % 2
        if h + 1 < HEADS and not (TRUNC == "p1ab" and AB_N < 0):
            load_head_slices(h + 1)
        for kw in range(16 if not (TRUNC == "p1ab" and AB_N < 1) else 0):
            bank = psA[:, (kw % 2) * 512:(kw % 2 + 1) * 512]
            def mmk(e, kw=kw, b=b, bank=bank):
                r = None
                for latc in range(4):
                    r = e.matmul(bank[:, :],
                                 kuk_h[:, b * 512 + latc * 128:][:, 0:128],
                                 kv_latT[:, latc * S + kw * 512:][:, 0:512],
                                 start=(latc == 0), stop=(latc == 3))
                return r
            w = [(dG, wslice_d[b])]
            pc = st["psA_cons"].get(kw % 2)
            if pc is not None:
                w.append(pc)
            emit("tensor", mmk, waits=w, inc=cP)
            k_cP = cP.v
            emit("scalar", (lambda e, kw=kw, bank=bank: e.copy(
                ktc[:, kw * 512:(kw + 1) * 512], bank[:, :])),
                waits=[(cP, k_cP)], inc=cA)
            st["psA_cons"][kw % 2] = (cA, cA.v)
        ktc_dA = cA.v
        for kcg in range(16 if not (TRUNC == "p1ab" and AB_N < 2) else 0):
            bank = ps_O[:, (kcg % 4) * 512:(kcg % 4 + 1) * 512]
            def mmv(e, kcg=kcg, b=b, bank=bank):
                r = None
                for j in range(4):
                    kc = kcg * 4 + j
                    for latc in range(4):
                        r = e.matmul(bank[:, j * 128:(j + 1) * 128],
                                     kv_latT[:, latc * S + kc * 128:][:, 0:128],
                                     kuv_h[:, b * 512 + latc * 128:][:, 0:128],
                                     start=(latc == 0), stop=(latc == 3))
                return r
            w = [(cV, st["oevac"][0]), (cV, st["oevac"][1])]
            pc = psO_cons.get(kcg % 4)
            if pc is not None:
                w.append(pc)
            emit("tensor", mmv, waits=w, inc=cP)
            v_cP = cP.v
            emit("scalar", (lambda e, kcg=kcg, bank=bank: e.copy(
                v_sb[:, kcg * 512:(kcg + 1) * 512], bank[:, :])),
                waits=[(cP, v_cP)], inc=cA)
            psO_cons[kcg % 4] = (cA, cA.v)
        v_dA = cA.v
        for qw in range(4 if not (TRUNC == "p1ab" and AB_N < 3) else 0):
            bank = psA[:, (qw % 2) * 512:(qw % 2 + 1) * 512]
            def mmq(e, qw=qw, b=b, bank=bank):
                r = None
                for qlc in range(8):
                    r = e.matmul(bank[:, :],
                                 qu_h[:, b * 1024 + qlc * 128:][:, 0:128],
                                 q_latT[:, qlc * TQ + qw * 512:][:, 0:512],
                                 start=(qlc == 0), stop=(qlc == 7))
                return r
            w = []
            pc = st["psA_cons"].get(qw % 2)
            if pc is not None:
                w.append(pc)
            emit("tensor", mmq, waits=w, inc=cP)
            q_cP = cP.v
            emit("scalar", (lambda e, qw=qw, bank=bank: e.copy(
                qtc[:, qw * 512:(qw + 1) * 512], bank[:, :])),
                waits=[(cP, q_cP)], inc=cA)
            st["psA_cons"][qw % 2] = (cA, cA.v)
        qtc_dA = cA.v
        if TRUNC == "p1ab" and h == 0:
            src_kt = ktc if AB_N >= 1 else k_rT
            src_v = v_sb if AB_N >= 2 else k_rT
            src_qt = qtc if AB_N >= 3 else qt_r
            emit("scalar", (lambda e, src_kt=src_kt: e.dma_start(
                dbg_kt[:, :], src_kt[:, :])),
                waits=[(cA, cA.v), (cP, cP.v), (cV, cV.v)], inc=dA, by=16)
            emit("scalar", (lambda e, src_v=src_v: e.dma_start(
                dbg_v[:, :], src_v[:, :])), inc=dA, by=16)
            emit("scalar", (lambda e, src_qt=src_qt: e.dma_start(
                dbg_qt[:, :], src_qt[:, :])), inc=dA, by=16)
            return finish_trunc()
        emit_gamma_tiles(0 if NO_GAMMA else (4 if h == 0 else 13))

        for qh in range(2):
            ov = ps_O[:, qh * 1024:(qh + 1) * 1024]
            qoff = qh * 1024
            for kc in range(NKC):
                sb_ = kc % 2
                sview = psA[:, sb_ * 1024:(sb_ + 1) * 1024]
                def mm_s(e, kc=kc, sview=sview, qoff=qoff):
                    r = None
                    for qw in range(2):
                        bv = sview[:, qw * 512:(qw + 1) * 512]
                        r = e.matmul(bv, ktc[:, kc * 128:(kc + 1) * 128],
                                     qtc[:, qoff + qw * 512:][:, 0:512],
                                     start=True, stop=False)
                        r = e.matmul(bv, k_rT[:, kc * 128:(kc + 1) * 128],
                                     qt_r[:, qoff + qw * 512:][:, 0:512],
                                     start=False, stop=True)
                    return r
                w = []
                if kc == 0:
                    w = [(cA, cA.v), (cV, cV.v), (cG, cG.v)]
                if kc >= 2:
                    w.append((cA, exp_c[(h, qh, kc - 2)]))
                emit("tensor", mm_s, waits=w, inc=(cP if kc == 0 else None))
                if kc >= 1:
                    pb_ = (kc - 1) % 2
                    def mm_av(e, kc=kc, ov=ov, pb_=pb_):
                        r = None
                        for qw in range(2):
                            r = e.matmul(ov[:, qw * 512:(qw + 1) * 512],
                                         v_sb[:, (kc - 1) * 128:kc * 128],
                                         pt[:, pb_ * 1024 + qw * 512:][:, 0:512],
                                         start=(kc - 1 == 0), stop=False)
                        return r
                    wv = [(cA, exp_c[(h, qh, kc - 1)])]
                    if kc == 1:
                        wv.append((cV, st["oevac"][qh]))
                        for k_ in (qh * 2, qh * 2 + 1):
                            pc = psO_cons.get(k_)
                            if pc is not None:
                                wv.append(pc)
                    emit("tensor", mm_av, waits=wv, inc=cP)
                iter_c[(h, qh, kc)] = cP.v
                def f_exp(e, sview=sview, sb_=sb_):
                    e.activation(pt[:, sb_ * 1024:sb_ * 1024 + 512],
                                 sview[:, 0:512], AF.Exp, scale=0.0625)
                    return e.activation(pt[:, sb_ * 1024 + 512:sb_ * 1024 + 1024],
                                        sview[:, 512:1024], AF.Exp, scale=0.0625)
                w = [(cP, iter_c[(h, qh, kc)])]
                if kc >= 2:
                    w += [(cV, sv_c[(h, qh, kc - 2)]), (cG, sg_c[(h, qh, kc - 2)])]
                emit("scalar", f_exp, waits=w, inc=cA)
                exp_c[(h, qh, kc)] = cA.v
                if kc == 0:
                    emit("vector", (lambda e, sb_=sb_: e.tensor_copy(
                        acc[:, 0:512], pt[:, sb_ * 1024:][:, 0:512])),
                        waits=[(cA, exp_c[(h, qh, kc)])], inc=cV)
                    emit("gpsimd", (lambda e, sb_=sb_: e.tensor_copy(
                        acc[:, 512:1024], pt[:, sb_ * 1024 + 512:][:, 0:512])),
                        waits=[(cA, exp_c[(h, qh, kc)])], inc=cG)
                else:
                    emit("vector", (lambda e, sb_=sb_: e.tensor_add(
                        acc[:, 0:512], acc[:, 0:512], pt[:, sb_ * 1024:][:, 0:512])),
                        waits=[(cA, exp_c[(h, qh, kc)])], inc=cV)
                    emit("gpsimd", (lambda e, sb_=sb_: e.tensor_add(
                        acc[:, 512:1024], acc[:, 512:1024],
                        pt[:, sb_ * 1024 + 512:][:, 0:512])),
                        waits=[(cA, exp_c[(h, qh, kc)])], inc=cG)
                sv_c[(h, qh, kc)] = cV.v
                sg_c[(h, qh, kc)] = cG.v
            def mm_av63(e, ov=ov):
                r = None
                for qw in range(2):
                    r = e.matmul(ov[:, qw * 512:(qw + 1) * 512],
                                 v_sb[:, (NKC - 1) * 128:NKC * 128],
                                 pt[:, ((NKC - 1) % 2) * 1024 + qw * 512:][:, 0:512],
                                 start=False, stop=True)
                return r
            emit("tensor", mm_av63, waits=[(cA, exp_c[(h, qh, NKC - 1)])], inc=cP)
            av_end = cP.v
            def mm_sum(e):
                e.matmul(psA[0:1, 0:512], onec[:, :], acc[:, 0:512],
                         start=True, stop=True)
                return e.matmul(psA[0:1, 512:1024], onec[:, :], acc[:, 512:1024],
                                start=True, stop=True)
            emit("tensor", mm_sum,
                 waits=[(cV, sv_c[(h, qh, NKC - 1)]), (cG, sg_c[(h, qh, NKC - 1)]),
                        (cA, exp_c[(h, qh, NKC - 1)])], inc=cP)
            sum_cP = cP.v
            emit("vector", lambda e: e.reciprocal(inv_row[0:1, 0:512],
                                                  psA[0:1, 0:512]),
                 waits=[(cP, sum_cP)])
            emit("vector", lambda e: e.reciprocal(inv_row[0:1, 512:1024],
                                                  psA[0:1, 512:1024]), inc=cV)
            rec_cV = cV.v
            st["psA_cons"][0] = (cV, rec_cV)
            st["psA_cons"][1] = (cV, rec_cV)
            def mm_bc(e):
                e.matmul(psA[:, 1024:1536], oner[:, :], inv_row[0:1, 0:512],
                         start=True, stop=True)
                return e.matmul(psA[:, 1536:2048], oner[:, :],
                                inv_row[0:1, 512:1024], start=True, stop=True)
            emit("tensor", mm_bc, waits=[(cV, rec_cV)], inc=cP)
            bc_cP = cP.v
            emit("scalar", lambda e: e.copy(inv_bc[:, 0:512], psA[:, 1024:1536]),
                 waits=[(cP, bc_cP)])
            emit("scalar", lambda e: e.copy(inv_bc[:, 512:1024], psA[:, 1536:2048]),
                 inc=cA)
            bce_cA = cA.v
            st["psA_cons"][2] = (cA, bce_cA)
            st["psA_cons"][3] = (cA, bce_cA)
            emit("vector", (lambda e, ov=ov: e.tensor_mul(
                osb[:, 0:512], ov[:, 0:512], inv_bc[:, 0:512])),
                waits=[(cP, av_end), (cA, bce_cA), (dA, st["osb_dma"])])
            emit("vector", (lambda e, ov=ov: e.tensor_mul(
                osb[:, 512:1024], ov[:, 512:1024], inv_bc[:, 512:1024])), inc=cV)
            st["oevac"][qh] = cV.v
            if not NO_OTSPILL:
                emit("scalar", (lambda e, h=h, qh=qh: e.dma_start(
                    ot_dram[h, :, qh * 1024:(qh + 1) * 1024], osb[:, :])),
                    waits=[(cV, st["oevac"][qh])], inc=dA, by=16)
                st["osb_dma"] = dA.v
            psO_cons[qh * 2] = (cP, av_end)
            psO_cons[qh * 2 + 1] = (cP, av_end)
        if TRUNC == f"p1h{h + 1}":
            return finish_trunc()

    g3b_dA = emit_gamma_final()
    if TRUNC == "p1":
        return finish_trunc()

    wout_d = 0
    for hc in range(16):
        def f(e, hc=hc):
            return e.dma_start(wout_sb[:, hc * D:(hc + 1) * D],
                               w_out[hc * 128:(hc + 1) * 128, :])
        emit("gpsimd", f, waits=[(cP, cP.v)], inc=dG, by=16)
    wout_d = dG.v

    B.fence_all()
    ot_done_dA = dA.v
    psO_g.__exit__(None, None, None)
    psTR2_g = nc.psum_tensor("psTR2", [128, 2048], BF16)
    TRH["t"] = psTR2_g.__enter__()
    psX_g = nc.psum_tensor("psX", [128, 1024], F32)
    psX = psX_g.__enter__()
    SQD["t"] = psX
    st["trb_cons"] = [0, 0]

    x1_spill = []
    nT_done = []
    otqc_cons = [0, 0]
    xq_cons = [0, 0]

    for qc in range(16):
        b = qc % 2
        ov_ = otqc[:, b * 2048:(b + 1) * 2048]
        xv_ = xq_sb[:, b * D:(b + 1) * D]
        def ldot(e, ov_=ov_, qc=qc):
            src = ot_dram[:, :, qc * 128:(qc + 1) * 128].rearrange("n p c -> p n c")
            return e.dma_start(ov_.rearrange("p (n c) -> p n c", n=16), src)
        emit("sync", ldot, waits=[(dA, ot_done_dA), (cP, otqc_cons[b])],
             inc=dS, by=16)
        ot_dS = dS.v
        emit("sync", (lambda e, xv_=xv_, qc=qc: e.dma_start(
            xv_[:, :], x_q[qc * 128:(qc + 1) * 128, :])),
            waits=[(cV, xq_cons[b])], inc=dS, by=16)
        xq_dS = dS.v
        def mmw(e, ov_=ov_):
            r = None
            for hc in range(16):
                for dmw in range(4):
                    r = e.matmul(psA[:, dmw * 512:(dmw + 1) * 512],
                                 ov_[:, hc * 128:(hc + 1) * 128],
                                 wout_sb[:, hc * D + dmw * 512:][:, 0:512],
                                 start=(hc == 0), stop=(hc == 15))
            return r
        w = [(dS, ot_dS), (dG, wout_d)]
        for k_ in range(4):
            pc = st["psA_cons"].get(k_)
            if pc is not None:
                w.append(pc)
        emit("tensor", mmw, waits=w, inc=cP)
        mmw_cP = cP.v
        otqc_cons[b] = mmw_cP
        for dmw in range(4):
            emit("vector", (lambda e, xv_=xv_, dmw=dmw: e.tensor_add(
                x1t[:, dmw * 512:(dmw + 1) * 512],
                psA[:, dmw * 512:(dmw + 1) * 512],
                xv_[:, dmw * 512:(dmw + 1) * 512])),
                waits=([(cP, mmw_cP), (dS, xq_dS),
                        (dA, x1_spill[-1] if x1_spill else 0)]
                       if dmw == 0 else ()),
                inc=(cV if dmw == 3 else None))
        x1_cV = cV.v
        xq_cons[b] = x1_cV
        for k_ in range(4):
            st["psA_cons"][k_] = (cV, x1_cV)
        emit("scalar", (lambda e, qc=qc: e.dma_start(
            x1_dram[qc, :, :], x1t[:, :])), waits=[(cV, x1_cV)], inc=dA, by=16)
        x1_spill.append(dA.v)
        if ATTN_ONLY:
            emit("scalar", (lambda e, qc=qc: e.dma_start(
                out_t[qc * 128:(qc + 1) * 128, :], x1t[:, :])), inc=dA, by=16)
            nT_done.append(cA.v)
            continue
        c0_, sq_cA_, inv_cA_ = ln_tile(x1t, h2t, qc % 2, [(cV, x1_cV)])
        h2_cV = ln_norm(x1t, h2t, c0_, inv_cA_, [])
        emit("vector", (lambda e, qc=qc: e.reduce_max(
            gammax[:, qc:qc + 1], h2t[:, :], AX, apply_absolute_value=True)))
        emit("vector", (lambda e, qc=qc: e.tensor_scalar(
            gammax[:, qc:qc + 1], gammax[:, qc:qc + 1], 1e-5, None, A.max)), inc=cV)
        gx_cV = cV.v
        emit("vector", (lambda e, qc=qc: e.reciprocal(
            smalls[:, 28:29], gammax[:, qc:qc + 1])), inc=cV)
        emit("vector", lambda e: e.tensor_scalar(
            trash[:, 0:1024], h2t[:, 0:1024], smalls[:, 28:29], 127.0,
            A.mult, A.mult))
        emit("vector", lambda e: e.tensor_scalar(
            ntok[:, 0:1024], trash[:, 0:1024], MAGIC, -MAGIC, A.add, A.add))
        emit("vector", lambda e: e.tensor_scalar(
            trash[:, 0:1024], h2t[:, 1024:2048], smalls[:, 28:29], 127.0,
            A.mult, A.mult))
        emit("vector", lambda e: e.tensor_scalar(
            ntok[:, 1024:2048], trash[:, 0:1024], MAGIC, -MAGIC, A.add, A.add),
            inc=cV)
        n_cV = cV.v
        emit("vector", (lambda e, qc=qc: e.tensor_scalar(
            cg_sc[:, qc:qc + 1], gammax[:, qc:qc + 1], g3b[:, 0:1], 1.0 / 127.0,
            A.mult, A.mult)), waits=[(cA, g3b_dA)])
        emit("vector", (lambda e, qc=qc: e.tensor_scalar(
            cv_sc[:, qc:qc + 1], gammax[:, qc:qc + 1], g3b[:, 1:2], 1.0 / 127.0,
            A.mult, A.mult)), inc=cV)
        nT_done.append(transpose_tile(ntok, nT, 0, TQ, qc * 128, n_cV))

    if TRUNC == "p2":
        return finish_trunc()
    if not ATTN_ONLY:
        B.fence_all()
        nat_cons = [0, 0, 0, 0]
        qn_cons = [0, 0, 0, 0]
        natb = [natst[:, 0:D], natst[:, D:2 * D],
                natst2[:, 0:D], natst2[:, D:2 * D]]
        qnb = [qnat[:, 0:D], qnat[:, D:2 * D],
               qnat2[:, 0:D], qnat2[:, D:2 * D]]
        bias_cons = [0, 0]
        brow_cons = [0]
        psF_cons = {}
        gv_spill = {}

        def quantize_nat(nv, qv, inv_col, extra_w):
            f = inv_col - 3
            emit("vector", (lambda e, nv=nv, f=f: e.tensor_scalar(
                trash[:, 0:1024], nv[:, 0:1024], g3b[:, 9 + f:10 + f],
                None, A.is_lt)), waits=extra_w)
            emit("vector", (lambda e, nv=nv, qv=qv, f=f: e.scalar_tensor_tensor(
                qv[:, 0:1024], nv[:, 0:1024], g3b[:, 6 + f:7 + f],
                trash[:, 0:1024], A.is_gt, A.subtract)))
            emit("vector", (lambda e, nv=nv, f=f: e.tensor_scalar(
                trash[:, 0:1024], nv[:, 1024:2048], g3b[:, 9 + f:10 + f],
                None, A.is_lt)))
            emit("vector", (lambda e, nv=nv, qv=qv, f=f: e.scalar_tensor_tensor(
                qv[:, 1024:2048], nv[:, 1024:2048], g3b[:, 6 + f:7 + f],
                trash[:, 0:1024], A.is_gt, A.subtract)), inc=cV)
            return cV.v

        def build_wT(hw, dstH, cb16, wdram, inv_col):
            slots = []
            for i in range(4):
                nb = st["wq"] % 4
                st["wq"] += 1
                nv = natb[nb]
                def ld(e, nv=nv, wdram=wdram, hw=hw, i=i):
                    return e.dma_start(
                        nv[:, :], wdram[hw * 512 + i * 128:hw * 512 + (i + 1) * 128, :])
                emit("sync", ld, waits=[(cV, nat_cons[nb])], inc=dS, by=16)
                slots.append((nb, dS.v))
            last = 0
            for i in range(4):
                nb, n_dS = slots[i]
                q_cV = quantize_nat(natb[nb], qnb[nb], inv_col,
                                    [(dS, n_dS), (cA, g3b_dA), (cP, qn_cons[nb])])
                nat_cons[nb] = q_cV
                last = transpose_tile(qnb[nb], dstH, cb16, 512, i * 128, q_cV)
                qn_cons[nb] = cP.v
            return last

        for hw in range(16):
            wb_ = hw % 2
            gT = gwT[:, wb_ * 8192:(wb_ + 1) * 8192]
            vT = vwT[:, wb_ * 8192:(wb_ + 1) * 8192]
            gT_dA = build_wT(hw, gwT, wb_ * 16, gw, 3)
            vT_dA = build_wT(hw, vwT, wb_ * 16, vw, 4)
            if TRUNC == "f1w":
                if hw == 0:
                    emit("gpsimd", lambda e: e.dma_start(dbg_g3[:, :], g3b[:, :]),
                         waits=[(cA, g3b_dA)], inc=dG, by=16)
                emit("gpsimd", (lambda e, hw=hw, gT=gT: e.dma_start(
                    dbg_gt[hw], gT[:, :])), waits=[(cA, gT_dA)], inc=dG, by=16)
                emit("gpsimd", (lambda e, hw=hw, vT=vT: e.dma_start(
                    dbg_vt[hw], vT[:, :])), waits=[(cA, vT_dA)], inc=dG, by=16)
            emit("sync", (lambda e, hw=hw: e.dma_start(
                brow[0:1, 0:512], gbias[0:1, hw * 512:(hw + 1) * 512])),
                waits=[(cP, brow_cons[0])], inc=dS, by=16)
            emit("sync", (lambda e, hw=hw: e.dma_start(
                brow[0:1, 512:1024], vbias[0:1, hw * 512:(hw + 1) * 512])),
                inc=dS, by=16)
            br_dS = dS.v
            def bmm(e):
                e.matmul(psX[:, 0:512], oner[:, :], brow[0:1, 0:512],
                         start=True, stop=True)
                return e.matmul(psX[:, 512:1024], oner[:, :], brow[0:1, 512:1024],
                                start=True, stop=True)
            emit("tensor", bmm, waits=[(cA, bias_cons[wb_]), (dS, br_dS)], inc=cP)
            bmm_cP = cP.v
            brow_cons[0] = bmm_cP
            emit("scalar", (lambda e, wb_=wb_: e.copy(
                biasb[:, wb_ * 1024:wb_ * 1024 + 512], psX[:, 0:512])),
                waits=[(cP, bmm_cP)])
            emit("scalar", (lambda e, wb_=wb_: e.copy(
                biasb[:, wb_ * 1024 + 512:wb_ * 1024 + 1024], psX[:, 512:1024])),
                inc=cA)
            bias_dA = cA.v
            bias_cons[wb_] = bias_dA
            for tc in range(16):
                gbank = psA[:, (tc % 2) * 512:(tc % 2 + 1) * 512]
                vbank = psA[:, 1024 + (tc % 2) * 512:1024 + (tc % 2 + 1) * 512]
                def mmgv(e, tc=tc, gT=gT, vT=vT, gbank=gbank, vbank=vbank):
                    r = None
                    for dc in range(16):
                        lhs = nT[:, dc * TQ + tc * 128:][:, 0:128]
                        r = e.matmul(gbank, lhs, gT[:, dc * 512:(dc + 1) * 512],
                                     start=(dc == 0), stop=(dc == 15))
                        r = e.matmul(vbank, lhs, vT[:, dc * 512:(dc + 1) * 512],
                                     start=(dc == 0), stop=(dc == 15))
                    return r
                w = [(cA, gT_dA), (cA, vT_dA), (cA, nT_done[tc])]
                for k_ in ((tc % 2), 2 + (tc % 2)):
                    pc = psF_cons.get(k_)
                    if pc is not None:
                        w.append(pc)
                emit("tensor", mmgv, waits=w, inc=cP)
                gv_cP = cP.v
                g1 = gvt[:, 0:512]
                gate = gvt[:, 512:1024]
                v1 = gvt[:, 1024:1536]
                gvo = gvt[:, 1536:2048]
                emit("vector", (lambda e, gbank=gbank, tc=tc, wb_=wb_, g1=g1:
                                e.scalar_tensor_tensor(
                                    g1[:, :], gbank, cg_sc[:, tc:tc + 1],
                                    biasb[:, wb_ * 1024:wb_ * 1024 + 512],
                                    A.mult, A.add)),
                     waits=[(cP, gv_cP), (cA, bias_dA),
                            (dA, gv_spill.get((hw - 1, tc), 0))], inc=cV)
                g1_cV = cV.v
                emit("scalar", (lambda e, g1=g1, gate=gate: e.activation(
                    gate[:, :], g1[:, :], AF.Exp, scale=-1.0)),
                    waits=[(cV, g1_cV)], inc=cA)
                si_cA = cA.v
                emit("vector", (lambda e, gate=gate: e.tensor_scalar(
                    gate[:, :], gate[:, :], 1.0, None, A.add)),
                    waits=[(cA, si_cA)])
                emit("vector", (lambda e, gate=gate: e.reciprocal(
                    gate[:, :], gate[:, :])))
                emit("vector", (lambda e, gate=gate, g1=g1: e.tensor_mul(
                    gate[:, :], gate[:, :], g1[:, :])), inc=cV)
                si_cA = None
                emit("vector", (lambda e, vbank=vbank, tc=tc, wb_=wb_, v1=v1:
                                e.scalar_tensor_tensor(
                                    v1[:, :], vbank, cv_sc[:, tc:tc + 1],
                                    biasb[:, wb_ * 1024 + 512:wb_ * 1024 + 1024],
                                    A.mult, A.add)), inc=cV)
                v1_cV = cV.v
                psF_cons[(tc % 2)] = (cV, g1_cV)
                psF_cons[2 + (tc % 2)] = (cV, v1_cV)
                emit("vector", (lambda e, gate=gate, v1=v1, gvo=gvo: e.tensor_mul(
                    gvo[:, :], gate[:, :], v1[:, :])),
                    waits=[(dA, gv_spill.get((hw, tc - 1),
                                             gv_spill.get((hw - 1, 15), 0)))])
                emit("vector", (lambda e, gvo=gvo, tc=tc, hw=hw: e.reduce_max(
                    gvmax[:, tc * 16 + hw:tc * 16 + hw + 1], gvo[:, :], AX,
                    apply_absolute_value=True)), inc=cV)
                gv_cV = cV.v
                emit("scalar", (lambda e, gvo=gvo, tc=tc, hw=hw: e.dma_start(
                    gv_dram[tc, :, hw * 512:(hw + 1) * 512], gvo[:, :])),
                    waits=[(cV, gv_cV)], inc=dA, by=16)
                gv_spill[(hw, tc)] = dA.v

        if TRUNC in ("f1", "f1w"):
            return finish_trunc()
        B.fence_all()
        natfb_cons = [0, 0, 0, 0]
        qnfb_cons = [0, 0]
        nfb = [natfb[:, 0:D], natfb[:, D:2 * D],
               natfb2[:, 0:D], natfb2[:, D:2 * D]]
        qfb = [qnfb, qnfb2]
        gvrd_cons = [0, 0, 0, 0]
        x1rd_cons = [0, 0]
        outb_cons = [0, 0]
        n2T_cons = 0
        psF_cons = {}

        for tc in range(16):
            emit("vector", (lambda e, tc=tc: e.reduce_max(
                smalls[:, 30:31], gvmax[:, tc * 16:(tc + 1) * 16], AX)))
            emit("vector", (lambda e, tc=tc: e.tensor_scalar(
                gvmax[:, tc * 16:tc * 16 + 1], smalls[:, 30:31], 1e-5, None, A.max)))
            emit("vector", (lambda e, tc=tc: e.tensor_scalar(
                c3_sc[:, tc:tc + 1], gvmax[:, tc * 16:tc * 16 + 1], g3b[:, 2:3],
                1.0 / 127.0, A.mult, A.mult)))
            emit("vector", (lambda e, tc=tc: e.reciprocal(
                smalls[:, 31:32], gvmax[:, tc * 16:tc * 16 + 1])))
            emit("vector", (lambda e, tc=tc: e.tensor_scalar(
                gvmax[:, tc * 16 + 1:tc * 16 + 2], smalls[:, 31:32], 127.0, None,
                A.mult)), inc=cV)
        c3_cV = cV.v

        for half in range(2):
            phase_dA, phase_cV = dA.v, cV.v
            for i in range(8):
                slots = []
                for s4 in range(4):
                    t_ = i * 4 + s4
                    nb = t_ % 4
                    nv = nfb[nb]
                    def ld(e, nv=nv, half=half, i=i, s4=s4):
                        return e.dma_start(
                            nv[:, :],
                            ow[half * 1024 + i * 128:half * 1024 + (i + 1) * 128,
                               s4 * 2048:(s4 + 1) * 2048])
                    w_ = [(cV, natfb_cons[nb])]
                    if i == 0:
                        w_ += [(dA, phase_dA), (cV, phase_cV)]
                    emit("sync", ld, waits=w_, inc=dS, by=16)
                    slots.append((nb, dS.v))
                for s4 in range(4):
                    t_ = i * 4 + s4
                    nb, n_dS = slots[s4]
                    qb = t_ % 2
                    qv_ = qfb[qb]
                    q_cV = quantize_nat(nfb[nb], qv_, 5,
                                        [(dS, n_dS), (cA, g3b_dA),
                                         (cP, qnfb_cons[qb]),
                                         (cP, cP.v if t_ < 2 and half == 0 else 0)])
                    natfb_cons[nb] = q_cV
                    for dc2 in range(2):
                        bk = st["trb"]
                        st["trb"] ^= 1
                        pb = TRH["t"][:, bk * 1024:(bk + 1) * 1024]
                        def tr(e, pb=pb, dc2=dc2, qv_=qv_):
                            r = None
                            for j in range(8):
                                dc = dc2 * 8 + j
                                r = e.transpose(pb[:, j * 128:(j + 1) * 128],
                                                qv_[:, dc * 128:(dc + 1) * 128],
                                                idn[:, :])
                            return r
                        emit("tensor", tr, waits=[(cV, q_cV),
                                                  (cA, st["trb_cons"][bk])], inc=cP)
                        tr_cP = cP.v
                        def ev(e, pb=pb, dc2=dc2, s4=s4, i=i):
                            hcb = s4 * 16 + dc2 * 8
                            dst = w3T[:, :].rearrange("p (n c) -> p n c", c=1024)[
                                :, hcb:hcb + 8, i * 128:(i + 1) * 128]
                            return e.copy(dst, pb[:, :].rearrange(
                                "p (n c) -> p n c", n=8))
                        emit("scalar", ev, waits=[(cP, tr_cP)], inc=cA)
                        st["trb_cons"][bk] = cA.v
                    qnfb_cons[qb] = cP.v
            w3_dA = cA.v
            emit("sync", (lambda e, half=half: e.dma_start(
                brow[0:1, 0:1024], obias[0:1, half * 1024:(half + 1) * 1024])),
                waits=[(cP, cP.v)], inc=dS, by=16)
            b3_dS = dS.v
            def bmm3(e):
                e.matmul(psX[:, 0:512], oner[:, :], brow[0:1, 0:512],
                         start=True, stop=True)
                return e.matmul(psX[:, 512:1024], oner[:, :], brow[0:1, 512:1024],
                                start=True, stop=True)
            emit("tensor", bmm3, waits=[(dS, b3_dS), (cA, cA.v)], inc=cP)
            bmm3_cP = cP.v
            emit("scalar", lambda e: e.copy(b3bc[:, 0:512], psX[:, 0:512]),
                 waits=[(cP, bmm3_cP)])
            emit("scalar", lambda e: e.copy(b3bc[:, 512:1024], psX[:, 512:1024]),
                 inc=cA)
            b3_dA = cA.v

            gv_ld = {}
            gidx = [0]

            def emit_gvload():
                idx = gidx[0]
                gidx[0] += 1
                tc_, h8 = idx // 8, idx % 8
                if tc_ >= 16:
                    return
                slot = idx % 4
                w_ = [(cV, gvrd_cons[slot]), (dA, gv_spill[(15, tc_)])]
                if idx < 4:
                    w_.append((cA, w3_dA))
                def ld(e, slot=slot, tc_=tc_, h8=h8):
                    return e.dma_start(
                        gvrd[:, slot * 1024:(slot + 1) * 1024],
                        gv_dram[tc_, :, h8 * 1024:(h8 + 1) * 1024])
                emit("sync", ld, waits=w_, inc=dS, by=16)
                gv_ld[idx] = dS.v

            x1_ld = {}

            def emit_x1load(tc_, half=half):
                if tc_ >= 16:
                    return
                b_ = tc_ % 2
                def ldx(e, tc_=tc_, half=half, b_=b_):
                    return e.dma_start(
                        x1rd[:, b_ * 1024:(b_ + 1) * 1024],
                        x1_dram[tc_, :, half * 1024:(half + 1) * 1024])
                emit("sync", ldx, waits=[(cV, x1rd_cons[b_]),
                                         (dA, x1_spill[tc_])], inc=dS, by=16)
                x1_ld[tc_] = dS.v

            for _ in range(4):
                emit_gvload()
            emit_x1load(0)
            emit_x1load(1)
            for tc in range(16):
                b = tc % 2
                n2_dA = 0
                for s4 in range(4):
                    iA = tc * 8 + s4 * 2
                    iB = iA + 1
                    gvA = gvrd[:, (iA % 4) * 1024:(iA % 4 + 1) * 1024]
                    gvB = gvrd[:, (iB % 4) * 1024:(iB % 4 + 1) * 1024]
                    emit("vector", (lambda e, gvA=gvA, tc=tc: e.tensor_scalar(
                        trash[:, 0:1024], gvA[:, :],
                        gvmax[:, tc * 16 + 1:tc * 16 + 2], None, A.mult)),
                        waits=[(dS, gv_ld[iB]), (cV, c3_cV), (cP, n2T_cons)])
                    emit("vector", lambda e: e.tensor_scalar(
                        n2q[:, 0:1024], trash[:, 0:1024], MAGIC, -MAGIC,
                        A.add, A.add))
                    emit("vector", (lambda e, gvB=gvB, tc=tc: e.tensor_scalar(
                        trash[:, 0:1024], gvB[:, :],
                        gvmax[:, tc * 16 + 1:tc * 16 + 2], None, A.mult)),
                        waits=[(dS, gv_ld.get(iB + 1, gv_ld[iB]))])
                    emit("vector", lambda e: e.tensor_scalar(
                        n2q[:, 1024:2048], trash[:, 0:1024], MAGIC, -MAGIC,
                        A.add, A.add), inc=cV)
                    q_cV = cV.v
                    gvrd_cons[iA % 4] = q_cV
                    gvrd_cons[iB % 4] = q_cV
                    emit_gvload()
                    emit_gvload()
                    for dc2 in range(2):
                        bk = st["trb"]
                        st["trb"] ^= 1
                        pb = TRH["t"][:, bk * 1024:(bk + 1) * 1024]
                        def tr(e, pb=pb, dc2=dc2):
                            r = None
                            for j in range(8):
                                dc = dc2 * 8 + j
                                r = e.transpose(pb[:, j * 128:(j + 1) * 128],
                                                n2q[:, dc * 128:(dc + 1) * 128],
                                                idn[:, :])
                            return r
                        emit("tensor", tr, waits=[(cV, q_cV),
                                                  (cA, st["trb_cons"][bk])], inc=cP)
                        tr_cP = cP.v
                        def ev(e, pb=pb, dc2=dc2, s4=s4):
                            hcb = s4 * 16 + dc2 * 8
                            dst = n2T[:, hcb * 128:(hcb + 8) * 128]
                            return e.copy(dst, pb[:, :])
                        emit("scalar", ev, waits=[(cP, tr_cP)], inc=cA)
                        st["trb_cons"][bk] = cA.v
                    n2_dA = cA.v
                xr = x1rd[:, b * 1024:(b + 1) * 1024]
                for jw in range(2):
                    bank = psA[:, (b * 2 + jw) * 512:(b * 2 + jw + 1) * 512]
                    def mm3(e, jw=jw, bank=bank):
                        r = None
                        for hc in range(64):
                            r = e.matmul(bank,
                                         n2T[:, hc * 128:(hc + 1) * 128],
                                         w3T[:, hc * 1024 + jw * 512:][:, 0:512],
                                         start=(hc == 0), stop=(hc == 63))
                        return r
                    w = [(cA, n2_dA), (cA, w3_dA)]
                    pc = psF_cons.get(b * 2 + jw)
                    if pc is not None:
                        w.append(pc)
                    emit("tensor", mm3, waits=w, inc=cP)
                    mm3_cP = cP.v
                    if jw == 1:
                        n2T_cons = mm3_cP
                    ob_ = outbuf[:, b * 1024 + jw * 512:b * 1024 + (jw + 1) * 512]
                    emit("vector", (lambda e, bank=bank, tc=tc, jw=jw, ob_=ob_:
                                    e.scalar_tensor_tensor(
                                        ob_[:, :], bank, c3_sc[:, tc:tc + 1],
                                        b3bc[:, jw * 512:(jw + 1) * 512],
                                        A.mult, A.add)),
                         waits=[(cP, mm3_cP), (cA, b3_dA),
                                (dA, max(outb_cons[0], outb_cons[1]))])
                    emit("vector", (lambda e, ob_=ob_, xr=xr, jw=jw: e.tensor_add(
                        ob_[:, :], ob_[:, :], xr[:, jw * 512:(jw + 1) * 512])),
                        waits=[(dS, x1_ld.get(tc + 1, x1_ld[tc]))], inc=cV)
                    psF_cons[b * 2 + jw] = (cV, cV.v)
                ev_cV = cV.v
                x1rd_cons[b] = ev_cV
                emit_x1load(tc + 2)
                def sto(e, tc=tc, half=half, b=b):
                    return e.dma_start(
                        out_t[tc * 128:(tc + 1) * 128,
                              half * 1024:(half + 1) * 1024],
                        outbuf[:, b * 1024:(b + 1) * 1024])
                emit("scalar", sto, waits=[(cV, ev_cV)], inc=dA, by=16)
                outb_cons[b] = dA.v

    emit("sync", lambda e: e.nop(),
         waits=[(dA, dA.v), (dS, dS.v), (dG, dG.v), (cP, cP.v), (cA, cA.v),
                (cV, cV.v), (cG, cG.v)])

    return B.finish()


_ROPE_CACHE = {}


def _rope_tables():
    if "k" in _ROPE_CACHE:
        return _ROPE_CACHE["k"]
    d = 128
    inv_freq = (np.float32(1.0) /
                (np.float32(10000.0) **
                 (np.arange(0, d, 2, dtype=np.float32) / np.float32(d))))
    ang = np.arange(S, dtype=np.float32)[:, None] * inv_freq[None, :]
    cos = np.cos(ang).astype(np.float32)
    sin = np.sin(ang).astype(np.float32)
    C = np.repeat(cos.T, 2, axis=0)
    Sg = np.empty((128, S), np.float32)
    Sg[0::2] = -sin.T
    Sg[1::2] = sin.T
    _ROPE_CACHE["k"] = (np.ascontiguousarray(C), np.ascontiguousarray(Sg))
    return _ROPE_CACHE["k"]


def kernel(L0_attn_x, L0_attn_w_kv_down, L0_attn_w_kv_up_k, L0_attn_w_kv_up_v,
           L0_attn_w_q_down, L0_attn_w_q_up, L0_attn_w_k_rope, L0_attn_w_q_rope,
           L0_attn_w_out, L0_ffn_gate_w, L0_ffn_gate_b, L0_ffn_val_w,
           L0_ffn_val_b, L0_ffn_out_w, L0_ffn_out_b):
    x = np.asarray(L0_attn_x, dtype=np.float32)
    C, Sg = _rope_tables()
    common = {
        "w_kvd": np.asarray(L0_attn_w_kv_down, np.float32),
        "w_kuk": np.asarray(L0_attn_w_kv_up_k, np.float32),
        "w_kuv": np.asarray(L0_attn_w_kv_up_v, np.float32),
        "w_qd": np.asarray(L0_attn_w_q_down, np.float32),
        "w_qu": np.asarray(L0_attn_w_q_up, np.float32),
        "w_kr": np.asarray(L0_attn_w_k_rope, np.float32),
        "w_qr": np.asarray(L0_attn_w_q_rope, np.float32),
        "w_out": np.asarray(L0_attn_w_out, np.float32),
        "gw": np.asarray(L0_ffn_gate_w, np.float32),
        "gbias": np.asarray(L0_ffn_gate_b, np.float32).reshape(1, HF),
        "vw": np.asarray(L0_ffn_val_w, np.float32),
        "vbias": np.asarray(L0_ffn_val_b, np.float32).reshape(1, HF),
        "ow": np.asarray(L0_ffn_out_w, np.float32),
        "obias": np.asarray(L0_ffn_out_b, np.float32).reshape(1, D),
        "ropec": C, "ropes": Sg,
        "ident": np.eye(128, dtype=bfloat16),
        "ones_col": np.ones((128, 1), np.float32),
        "ones_row": np.ones((1, 128), np.float32),
    }
    in_maps = []
    for c in range(8):
        be, qi = c // 4, c % 4
        q0 = qi * TQ
        m = dict(common)
        m["x_be"] = x[be]
        m["x_q"] = x[be, q0:q0 + TQ]
        m["ropeqc"] = np.ascontiguousarray(C[:, q0:q0 + TQ])
        m["ropeqs"] = np.ascontiguousarray(Sg[:, q0:q0 + TQ])
        in_maps.append(m)
    nc = build_program()
    res = run_bass_kernel_spmd(nc, in_maps, core_ids=list(range(8)))
    out = np.empty((2, S, D), np.float32)
    for c in range(8):
        be, qi = c // 4, c % 4
        out[be, qi * TQ:(qi + 1) * TQ] = res.results[c]["out"]
    return out
